# version 01
# baseline (speedup 1.0000x reference)
import numpy as np
import ml_dtypes
from contextlib import ExitStack
import concourse.bass as bass
import concourse.mybir as mybir
from concourse.bass_utils import run_bass_kernel_spmd

F32 = mybir.dt.float32
BF16 = mybir.dt.bfloat16
AF = mybir.ActivationFunctionType
ALU = mybir.AluOpType
AX = mybir.AxisListType
BF = ml_dtypes.bfloat16

D = 1024
NCORE = 8
TPC = 2048
NTT = TPC // 128
SQ = 8192


class Seq:

    def __init__(self, nc, ctx, chans):
        self.nc = nc
        self.sems = {c: ctx.enter_context(nc.semaphore(f"sem_{c}")) for c in chans}
        self.count = {c: 0 for c in chans}
        self.other_count = {c: {} for c in chans}
        self.last_eng = {c: None for c in chans}
        self.ops = []

    def op(self, eng, fn, chan="g", inc=None, after=(), force_wait=False):
        if inc is None:
            inc = 16 if eng in ("sync",) or getattr(fn, "_is_dma", False) else 1
        waits = []
        tgt = self.count[chan] if force_wait else self.other_count[chan].get(eng, 0)
        if tgt > 0:
            waits.append((chan, tgt))
        for m in after:
            if m is not None:
                waits.append(m)
        self.ops.append((eng, fn, chan, waits, inc))
        newc = self.count[chan] + inc
        for e2 in ("sync", "tensor", "vector", "scalar", "gpsimd"):
            if e2 != eng:
                self.other_count[chan][e2] = newc
        self.count[chan] = newc
        self.last_eng[chan] = eng

    def dma(self, fn, chan="g", after=(), eng="sync"):
        self.op(eng, fn, chan, inc=16, after=after)

    def mark(self, chan):
        return (chan, self.count[chan])

    def barrier_marks(self):
        return [self.mark(c) for c in self.count]

    def emit(self, blk):
        final = dict(self.count)

        def mk(name):
            def body(eng):
                lastw = {}
                for (e, fn, chan, waits, inc) in self.ops:
                    if e != name:
                        continue
                    for (c, v) in waits:
                        if v > 0 and lastw.get(c, -1) < v:
                            eng.wait_ge(self.sems[c], v)
                            lastw[c] = v
                    fn(eng).then_inc(self.sems[chan], inc)
                if name == "sync":
                    for c, v in final.items():
                        if v > 0:
                            eng.wait_ge(self.sems[c], v)
            return body

        for name in ("sync", "tensor", "vector", "scalar", "gpsimd"):
            blk.__getattr__(name)(mk(name)) if False else getattr(blk, name)(mk(name))


def ln_tile(s, x_tile, xn_bf_tile, st6, st2, tmp1, eps_col, chan="g"):
    s.op("vector", lambda e: e.bn_stats(st6[:, 0:6], x_tile[:, 0:512]), chan)
    s.op("vector", lambda e: e.bn_stats(st6[:, 6:12], x_tile[:, 512:1024]), chan)
    s.op("vector", lambda e: e.bn_aggr(st2[:], st6[:]), chan, force_wait=True)
    s.op("scalar", lambda e: e.activation(tmp1[:], st2[:, 1:2], AF.Sqrt, bias=eps_col[:]), chan, force_wait=True)
    s.op("vector", lambda e: e.reciprocal(tmp1[:], tmp1[:]), chan)
    s.op("vector", lambda e: e.tensor_scalar(xn_bf_tile[:], x_tile[:], st2[:, 0:1], tmp1[:],
                                             ALU.subtract, ALU.mult), chan, force_wait=True)


def load_weight_bf(s, hbm_ap, wsb, wtmp, kt, cols, chan="g"):
    src = hbm_ap.rearrange("(a p) n -> a p n", p=128)
    for a in range(kt):
        s.dma(lambda e, a=a: e.dma_start(wtmp[:, a * cols:(a + 1) * cols], src[a]), chan)
    s.op("vector", lambda e: e.tensor_copy(wsb[:], wtmp[:]), chan)



def build_qkv0():
    nc = bass.Bass("TRN2", target_bir_lowering=False)
    x = nc.dram_tensor("x", [TPC, D], F32, kind="ExternalInput").ap()
    wqkv = nc.dram_tensor("wqkv", [D, 3 * D], F32, kind="ExternalInput").ap()
    qT = nc.dram_tensor("qT", [D, TPC], BF16, kind="ExternalOutput").ap()
    kT = nc.dram_tensor("kT", [D, TPC], BF16, kind="ExternalOutput").ap()
    v = nc.dram_tensor("v", [TPC, D], BF16, kind="ExternalOutput").ap()
    xnT = nc.dram_tensor("xnT", [D, TPC], BF16, kind="ExternalOutput").ap()

    with ExitStack() as ctx:
        def sb(n, shp, dt): return ctx.enter_context(nc.sbuf_tensor(n, shp, dt))
        def ps(n, shp, dt): return ctx.enter_context(nc.psum_tensor(n, shp, dt))
        s = Seq(nc, ctx, chans=("g", "w"))
        xt = sb("xt", [128, D], F32)
        st6 = sb("st6", [128, 12], F32)
        st2 = sb("st2", [128, 2], F32)
        tmp1 = sb("tmp1", [128, 1], F32)
        eps_col = sb("eps_col", [128, 1], F32)
        xn_tm = sb("xn_tm", [128, D], BF16)
        xnT_sb = sb("xnT_sb", [128, 8 * TPC], BF16)
        wsb = sb("wsb", [128, 8 * 3 * D], BF16)
        wtmp = sb("wtmp", [128, 8 * 3 * D], F32)
        out_bf = sb("out_bf", [128, 2 * 512], BF16)
        pst = [ps(f"ps{i}", [128, 512], F32) for i in range(2)]
        blk = ctx.enter_context(nc.Block())

        s.op("vector", lambda e: e.memset(eps_col[:], 1e-5), "g")
        load_weight_bf(s, wqkv, wsb, wtmp, 8, 3 * D, chan="w")
        wdone = s.mark("w")

        xr = x.rearrange("(t p) n -> t p n", p=128)
        for tt in range(NTT):
            s.dma(lambda e, tt=tt: e.dma_start(xt[:], xr[tt]), "g")
            ln_tile(s, xt, xn_tm, st6, st2, tmp1, eps_col, "g")
            for j in range(8):
                s.dma(lambda e, tt=tt, j=j: e.dma_start(
                    xnT_sb[:, j * TPC + tt * 128: j * TPC + (tt + 1) * 128],
                    xn_tm[:, j * 128:(j + 1) * 128], transpose=True), "g")
        xndone = s.mark("g")

        def proj(out_hbm, col0, scale, n_m, mode):
            outr = out_hbm.rearrange("(m p) n -> m p n", p=128)
            idx = 0
            for m in range(n_m):
                nchunks = 4 if mode != "v" else 2
                for nch in range(nchunks):
                    p = pst[idx % 2]
                    idx += 1
                    for k in range(8):
                        if mode == "v":
                            lhs = xnT_sb[:, k * TPC + m * 128:(k * TPC) + (m + 1) * 128]
                            rhs = wsb[:, k * 3 * D + 2 * D + nch * 512: k * 3 * D + 2 * D + (nch + 1) * 512]
                        else:
                            lhs = wsb[:, k * 3 * D + col0 + m * 128: k * 3 * D + col0 + (m + 1) * 128]
                            rhs = xnT_sb[:, k * TPC + nch * 512: k * TPC + (nch + 1) * 512]
                        s.op("tensor", lambda e, p=p, lhs=lhs, rhs=rhs, k=k: e.matmul(
                            p[:], lhs, rhs, start=(k == 0), stop=(k == 7)), "g",
                            after=[wdone, xndone] if k == 0 else ())
                    ob = out_bf[:, (idx % 2) * 512:(idx % 2) * 512 + 512]
                    if scale is None:
                        s.op("scalar", lambda e, p=p, ob=ob: e.copy(ob, p[:]), "g")
                    else:
                        s.op("scalar", lambda e, p=p, ob=ob: e.mul(ob, p[:], scale), "g")
                    s.dma(lambda e, m=m, nch=nch, ob=ob, outr=outr: e.dma_start(
                        outr[m][:, nch * 512:(nch + 1) * 512], ob), "g")

        proj(qT, 0, 1.0 / 32.0, 8, "qk")
        proj(kT, D, None, 8, "qk")
        proj(v, None, None, NTT, "v")

        xnTr = xnT.rearrange("(m p) n -> m p n", p=128)
        for j in range(8):
            s.dma(lambda e, j=j: e.dma_start(xnTr[j], xnT_sb[:, j * TPC:(j + 1) * TPC]), "g")
        s.emit(blk)
    return nc



def build_attn0():
    nc = bass.Bass("TRN2", target_bir_lowering=False)
    qT = nc.dram_tensor("qT", [D, TPC], BF16, kind="ExternalInput").ap()
    kT = nc.dram_tensor("kT", [D, SQ], BF16, kind="ExternalInput").ap()
    v = nc.dram_tensor("v", [SQ, D], BF16, kind="ExternalInput").ap()
    yT = nc.dram_tensor("yT", [D, TPC], BF16, kind="ExternalOutput").ap()

    NQC = TPC // 256
    NKC = SQ // 1024
    NT = NKC * 8
    with ExitStack() as ctx:
        def sb(n, shp, dt): return ctx.enter_context(nc.sbuf_tensor(n, shp, dt))
        def ps(n, shp, dt): return ctx.enter_context(nc.psum_tensor(n, shp, dt))
        s = Seq(nc, ctx, chans=("pe", "ex", "ev", "kv", "pre"))
        qT_sb = sb("qT_sb", [128, 8 * TPC], BF16)
        kch = [sb(f"kch{i}", [128, 8 * 1024], BF16) for i in range(2)]
        vch = [sb(f"vch{i}", [128, 8 * 1024], BF16) for i in range(2)]
        E_sb = [sb(f"E{i}", [128, 256], BF16) for i in range(2)]
        ones_col = sb("ones_col", [128, 1], BF16)
        den_sb = sb("den_sb", [1, 256], F32)
        bc_sb = sb("bc_sb", [128, 256], F32)
        ones_row = sb("ones_row", [1, 128], F32)
        yT_sb = sb("yT_sb", [128, 8 * TPC], BF16)
        y_ps = ps("y_ps", [128, 2048], F32)
        s_ps = [ps(f"s_ps{i}", [128, 256], F32) for i in range(2)]
        den_ps = ps("den_ps", [1, 256], F32)
        bc_ps = ps("bc_ps", [128, 256], F32)
        blk = ctx.enter_context(nc.Block())

        qTr = qT.rearrange("(m p) n -> m p n", p=128)
        kTr = kT.rearrange("(m p) n -> m p n", p=128)
        vr = v.rearrange("(c t p) n -> c t p n", p=128, t=8)
        yTr = yT.rearrange("(m p) n -> m p n", p=128)

        s.op("vector", lambda e: e.memset(ones_col[:], 1.0), "pre")
        s.op("vector", lambda e: e.memset(ones_row[:], 1.0), "pre")
        for j in range(8):
            s.dma(lambda e, j=j: e.dma_start(qT_sb[:, j * TPC:(j + 1) * TPC], qTr[j]), "pre")
        pre_done = s.mark("pre")

        kv_loaded = {}
        pe_chunk_done = {}
        exp_done = {}
        y_done = {}
        ev_done = {}

        def load_chunk(gi):
            qc, c = divmod(gi, NKC)
            buf = gi % 2
            after = [pre_done]
            if gi >= 2:
                after.append(pe_chunk_done[gi - 2])
            for j in range(8):
                s.dma(lambda e, j=j, c=c, buf=buf: e.dma_start(
                    kch[buf][:, j * 1024:(j + 1) * 1024],
                    kTr[j][:, c * 1024:(c + 1) * 1024]), "kv", after=after if j == 0 else ())
            for t in range(8):
                s.dma(lambda e, t=t, c=c, buf=buf: e.dma_start(
                    vch[buf][:, t * 1024:(t + 1) * 1024], vr[c, t]), "kv")
            kv_loaded[gi] = s.mark("kv")

        def scores(ti):
            qc, t = divmod(ti, NT)
            gi = qc * NKC + t // 8
            buf = (t // 8) % 2
            tl = t % 8
            after = [kv_loaded[gi]]
            if ti >= 2:
                after.append(exp_done[ti - 2])
            if t == 0 and qc > 0:
                after.append(ev_done[qc - 1])
            for k in range(8):
                s.op("tensor", lambda e, k=k, buf=buf, tl=tl, qc=qc, t=t: e.matmul(
                    s_ps[t % 2][:],
                    kch[buf][:, k * 1024 + tl * 128: k * 1024 + (tl + 1) * 128],
                    qT_sb[:, k * TPC + (qc % 8) * 256: k * TPC + (qc % 8) * 256 + 256],
                    start=(k == 0), stop=(k == 7)), "pe", after=after if k == 0 else ())

        def expf(ti):
            qc, t = divmod(ti, NT)
            after = [s.mark("pe_scores_%d" % ti) if False else None]
            after = [scores_done[ti]]
            if ti >= 2:
                after.append(y_done[ti - 2])
            s.op("scalar", lambda e, t=t: e.activation(E_sb[t % 2][:], s_ps[t % 2][:], AF.Exp),
                 "ex", after=after)

        def ymm(ti):
            qc, t = divmod(ti, NT)
            gi = qc * NKC + t // 8
            buf = (t // 8) % 2
            tl = t % 8
            after = [exp_done[ti]]
            for dv in range(8):
                s.op("tensor", lambda e, dv=dv, buf=buf, tl=tl, t=t: e.matmul(
                    y_ps[:, dv * 256:(dv + 1) * 256],
                    vch[buf][:, tl * 1024 + dv * 128: tl * 1024 + (dv + 1) * 128],
                    E_sb[t % 2][:],
                    start=(t == 0 and dv % 2 == 0), stop=(t == NT - 1), skip_group_check=True),
                    "pe", after=after if dv == 0 else ())
            s.op("tensor", lambda e, t=t: e.matmul(
                den_ps[:], ones_col[:], E_sb[t % 2][:],
                start=(t == 0), stop=(t == NT - 1), skip_group_check=True), "pe")

        scores_done = {}
        for qc in range(NQC):
            for ti in range(qc * NT, (qc + 1) * NT):
                t = ti % NT
                if t % 8 == 0:
                    gi = qc * NKC + t // 8
                    load_chunk(gi)
                scores(ti)
                scores_done[ti] = s.mark("pe")
                expf(ti)
                exp_done[ti] = s.mark("ex")
                ymm(ti)
                y_done[ti] = s.mark("pe")
                if t % 8 == 7:
                    pe_chunk_done[qc * NKC + t // 8] = s.mark("pe")
            s.op("vector", lambda e: e.reciprocal(den_sb[:], den_ps[:]), "ev",
                 after=[y_done[(qc + 1) * NT - 1]])
            rec_m = s.mark("ev")
            s.op("tensor", lambda e: e.matmul(bc_ps[:], ones_row[:], den_sb[:],
                                              start=True, stop=True), "pe", after=[rec_m])
            bc_m = s.mark("pe")
            s.op("scalar", lambda e: e.copy(bc_sb[:], bc_ps[:]), "ev", after=[bc_m])
            for dv in range(8):
                s.op("vector", lambda e, dv=dv, qc=qc: e.tensor_tensor(
                    yT_sb[:, dv * TPC + qc * 256: dv * TPC + (qc + 1) * 256],
                    y_ps[:, dv * 256:(dv + 1) * 256], bc_sb[:], ALU.mult), "ev")
            ev_done[qc] = s.mark("ev")

        for m in range(8):
            s.dma(lambda e, m=m: e.dma_start(yTr[m], yT_sb[:, m * TPC:(m + 1) * TPC]), "kv",
                  after=[ev_done[NQC - 1]])
        s.emit(blk)
    return nc



def build_mix0():
    nc = bass.Bass("TRN2", target_bir_lowering=False)
    x = nc.dram_tensor("x", [TPC, D], F32, kind="ExternalInput").ap()
    yT = nc.dram_tensor("yT", [D, TPC], BF16, kind="ExternalInput").ap()
    xnT = nc.dram_tensor("xnT", [D, TPC], BF16, kind="ExternalInput").ap()
    owT = nc.dram_tensor("owT", [D, D], F32, kind="ExternalInput").ap()
    outb = nc.dram_tensor("outb", [1, D], F32, kind="ExternalInput").ap()
    fwT = nc.dram_tensor("fwT", [D, D], F32, kind="ExternalInput").ap()
    fscale = nc.dram_tensor("fscale", [1, D], F32, kind="ExternalInput").ap()
    fbias = nc.dram_tensor("fbias", [1, D], F32, kind="ExternalInput").ap()
    wqT = nc.dram_tensor("wqT", [D, D], F32, kind="ExternalInput").ap()
    wkT = nc.dram_tensor("wkT", [D, D], F32, kind="ExternalInput").ap()
    wvT = nc.dram_tensor("wvT", [D, D], F32, kind="ExternalInput").ap()
    wgT = nc.dram_tensor("wgT", [D, 16], F32, kind="ExternalInput").ap()
    bg = nc.dram_tensor("bg", [16, 1], F32, kind="ExternalInput").ap()
    out1 = nc.dram_tensor("out1", [TPC, D], F32, kind="ExternalOutput").ap()
    xn2tm = nc.dram_tensor("xn2tm", [TPC, D], BF16, kind="ExternalOutput").ap()
    xn2T = nc.dram_tensor("xn2T", [D, TPC], BF16, kind="ExternalOutput").ap()
    qfT = nc.dram_tensor("qfT", [D, TPC], BF16, kind="ExternalOutput").ap()
    kfT = nc.dram_tensor("kfT", [D, TPC], BF16, kind="ExternalOutput").ap()
    v1 = nc.dram_tensor("v1", [TPC, D], BF16, kind="ExternalOutput").ap()
    gate = nc.dram_tensor("gate", [16, TPC], F32, kind="ExternalOutput").ap()

    with ExitStack() as ctx:
        def sb(n, shp, dt): return ctx.enter_context(nc.sbuf_tensor(n, shp, dt))
        def ps(n, shp, dt): return ctx.enter_context(nc.psum_tensor(n, shp, dt))
        s = Seq(nc, ctx, chans=("g", "w"))
        yT_sb = sb("yT_sb", [128, 8 * TPC], BF16)
        xnT_sb = sb("xnT_sb", [128, 8 * TPC], BF16)
        wa = sb("wa", [128, 8 * D], BF16)
        wf = sb("wf", [128, 8 * D], BF16)
        wtmp = sb("wtmp", [128, 8 * D], F32)
        rowtmp = sb("rowtmp", [1, D], F32)
        outb_bf = sb("outb_bf", [1, D], BF16)
        ones1_bf = sb("ones1_bf", [1, 128], BF16)
        scale_mat = sb("scale_mat", [128, D], F32)
        bias_mat = sb("bias_mat", [128, D], F32)
        xt = sb("xt", [128, D], F32)
        ffn_sb = sb("ffn_sb", [128, D], F32)
        o1_sb = sb("o1_sb", [128, D], F32)
        st6 = sb("st6", [128, 12], F32)
        st2 = sb("st2", [128, 2], F32)
        tmp1 = sb("tmp1", [128, 1], F32)
        eps_col = sb("eps_col", [128, 1], F32)
        xn2_sb = sb("xn2_sb", [128, D], BF16)
        xn2T_sb = sb("xn2T_sb", [128, 8 * TPC], BF16)
        ps_a = ps("ps_a", [128, 1024], F32)
        ps_f = ps("ps_f", [128, 1024], F32)
        ps_b = ps("ps_b", [128, 1024], F32)
        blk = ctx.enter_context(nc.Block())

        s.op("vector", lambda e: e.memset(eps_col[:], 1e-5), "g")
        yTr = yT.rearrange("(m p) n -> m p n", p=128)
        xnTr = xnT.rearrange("(m p) n -> m p n", p=128)
        for j in range(8):
            s.dma(lambda e, j=j: e.dma_start(yT_sb[:, j * TPC:(j + 1) * TPC], yTr[j]), "g")
            s.dma(lambda e, j=j: e.dma_start(xnT_sb[:, j * TPC:(j + 1) * TPC], xnTr[j]), "g")
        load_weight_bf(s, owT, wa, wtmp, 8, D, chan="g")
        src = fwT.rearrange("(a p) n -> a p n", p=128)
        for a in range(8):
            s.dma(lambda e, a=a: e.dma_start(wtmp[:, a * D:(a + 1) * D], src[a]), "g")
        s.op("vector", lambda e: e.tensor_scalar(wf[:], wtmp[:], 0.0, None, ALU.is_ge), "g")
        s.op("vector", lambda e: e.tensor_scalar(wf[:], wf[:], 2.0, -1.0, ALU.mult, ALU.add), "g", force_wait=True)
        s.op("vector", lambda e: e.memset(ones1_bf[:], 1.0), "g")
        s.dma(lambda e: e.dma_start(rowtmp[:], outb), "g")
        s.op("vector", lambda e: e.tensor_copy(outb_bf[:], rowtmp[:]), "g")
        s.dma(lambda e: e.dma_start(rowtmp[:], fscale), "g", eng="scalar")
        onesf = sb("onesf", [1, 128], F32)
        s.op("vector", lambda e: e.memset(onesf[:], 1.0), "g")
        for nch in range(2):
            s.op("tensor", lambda e, nch=nch: e.matmul(
                ps_b[:, nch * 512:(nch + 1) * 512], onesf[:], rowtmp[:, nch * 512:(nch + 1) * 512],
                start=True, stop=True), "g", force_wait=(nch == 0))
        s.op("vector", lambda e: e.tensor_copy(scale_mat[:], ps_b[:]), "g")
        s.dma(lambda e: e.dma_start(rowtmp[:], fbias), "g")
        for nch in range(2):
            s.op("tensor", lambda e, nch=nch: e.matmul(
                ps_b[:, nch * 512:(nch + 1) * 512], onesf[:], rowtmp[:, nch * 512:(nch + 1) * 512],
                start=True, stop=True), "g", force_wait=(nch == 0))
        s.op("vector", lambda e: e.tensor_copy(bias_mat[:], ps_b[:]), "g")
        pre = s.mark("g")

        xr = x.rearrange("(t p) n -> t p n", p=128)
        o1r = out1.rearrange("(t p) n -> t p n", p=128)
        xn2r = xn2tm.rearrange("(t p) n -> t p n", p=128)
        for tt in range(NTT):
            s.dma(lambda e, tt=tt: e.dma_start(xt[:], xr[tt]), "g")
            for nch in range(2):
                for k in range(8):
                    s.op("tensor", lambda e, k=k, nch=nch, tt=tt: e.matmul(
                        ps_a[:, nch * 512:(nch + 1) * 512],
                        yT_sb[:, k * TPC + tt * 128: k * TPC + (tt + 1) * 128],
                        wa[:, k * D + nch * 512: k * D + (nch + 1) * 512],
                        start=(k == 0), stop=False), "g", force_wait=(k == 0 and nch == 0))
                s.op("tensor", lambda e, nch=nch: e.matmul(
                    ps_a[:, nch * 512:(nch + 1) * 512], ones1_bf[:],
                    outb_bf[:, nch * 512:(nch + 1) * 512],
                    start=False, stop=True, skip_group_check=True), "g")
            for nch in range(2):
                for k in range(8):
                    s.op("tensor", lambda e, k=k, nch=nch, tt=tt: e.matmul(
                        ps_f[:, nch * 512:(nch + 1) * 512],
                        xnT_sb[:, k * TPC + tt * 128: k * TPC + (tt + 1) * 128],
                        wf[:, k * D + nch * 512: k * D + (nch + 1) * 512],
                        start=(k == 0), stop=(k == 7)), "g")
            s.op("vector", lambda e: e.tensor_tensor(ffn_sb[:], ps_f[:], scale_mat[:], ALU.mult), "g", force_wait=True)
            s.op("vector", lambda e: e.tensor_tensor(ffn_sb[:], ffn_sb[:], bias_mat[:], ALU.add), "g")
            s.op("vector", lambda e: e.tensor_scalar(ffn_sb[:], ffn_sb[:], 100.0, -100.0, ALU.min, ALU.max), "g")
            s.op("vector", lambda e: e.tensor_tensor(o1_sb[:], xt[:], ps_a[:], ALU.add), "g")
            s.op("vector", lambda e: e.tensor_tensor(o1_sb[:], o1_sb[:], ffn_sb[:], ALU.add), "g")
            s.dma(lambda e, tt=tt: e.dma_start(o1r[tt], o1_sb[:]), "g")
            ln_tile(s, o1_sb, xn2_sb, st6, st2, tmp1, eps_col, "g")
            s.dma(lambda e, tt=tt: e.dma_start(xn2r[tt], xn2_sb[:]), "g")
            for j in range(8):
                s.dma(lambda e, tt=tt, j=j: e.dma_start(
                    xn2T_sb[:, j * TPC + tt * 128: j * TPC + (tt + 1) * 128],
                    xn2_sb[:, j * 128:(j + 1) * 128], transpose=True), "g")
        xn2done = s.mark("g")
        xn2Tr = xn2T.rearrange("(m p) n -> m p n", p=128)
        for j in range(8):
            s.dma(lambda e, j=j: e.dma_start(xn2Tr[j], xn2T_sb[:, j * TPC:(j + 1) * TPC]), "g")

        t_f1 = sb("t_f1", [128, 512], F32)
        t_f2 = sb("t_f2", [128, 512], F32)
        t_f3 = sb("t_f3", [128, 512], F32)
        fm_bf = sb("fm_bf", [128, 512], BF16)

        def fproj(whbm, out_hbm, scale, featmap):
            load_weight_bf(s, whbm, wa, wtmp, 8, D, chan="g")
            outr = out_hbm.rearrange("(m p) n -> m p n", p=128)
            idx = 0
            for m in range(8):
                for nch in range(4):
                    p = [ps_a, ps_f][idx % 2]
                    idx += 1
                    for k in range(8):
                        s.op("tensor", lambda e, p=p, k=k, m=m, nch=nch: e.matmul(
                            p[:, 0:512],
                            wa[:, k * D + m * 128: k * D + (m + 1) * 128],
                            xn2T_sb[:, k * TPC + nch * 512: k * TPC + (nch + 1) * 512],
                            start=(k == 0), stop=(k == 7)), "g", force_wait=(k == 0))
                    if featmap:
                        if scale is not None:
                            s.op("scalar", lambda e, p=p: e.mul(t_f1[:], p[:, 0:512], scale), "g")
                        else:
                            s.op("scalar", lambda e, p=p: e.copy(t_f1[:], p[:, 0:512]), "g")
                        s.op("vector", lambda e: e.tensor_scalar(t_f2[:], t_f1[:], 0.0, None, ALU.min), "g")
                        s.op("scalar", lambda e: e.activation(t_f3[:], t_f2[:], AF.Exp), "g")
                        s.op("vector", lambda e: e.tensor_scalar(t_f2[:], t_f1[:], 0.0, None, ALU.max), "g")
                        s.op("vector", lambda e: e.tensor_tensor(fm_bf[:], t_f3[:], t_f2[:], ALU.add), "g")
                    else:
                        s.op("scalar", lambda e, p=p: e.copy(fm_bf[:], p[:, 0:512]), "g")
                    s.dma(lambda e, m=m, nch=nch, outr=outr: e.dma_start(
                        outr[m][:, nch * 512:(nch + 1) * 512], fm_bf[:]), "g")

        fproj(wqT, qfT, 0.125, True)
        fproj(wkT, kfT, None, True)
        load_weight_bf(s, wvT, wa, wtmp, 8, D, chan="g")
        v1r = v1.rearrange("(t p) n -> t p n", p=128)
        idx = 0
        for tt in range(NTT):
            for nch in range(2):
                p = [ps_a, ps_f][idx % 2]
                idx += 1
                for k in range(8):
                    s.op("tensor", lambda e, p=p, k=k, tt=tt, nch=nch: e.matmul(
                        p[:, 0:512],
                        xn2T_sb[:, k * TPC + tt * 128: k * TPC + (tt + 1) * 128],
                        wa[:, k * D + nch * 512: k * D + (nch + 1) * 512],
                        start=(k == 0), stop=(k == 7)), "g", force_wait=(k == 0))
                s.op("scalar", lambda e, p=p: e.copy(fm_bf[:], p[:, 0:512]), "g")
                s.dma(lambda e, tt=tt, nch=nch: e.dma_start(
                    v1r[tt][:, nch * 512:(nch + 1) * 512], fm_bf[:]), "g")
        wg_sb = sb("wg_sb", [128, 8 * 16], BF16)
        wg_tmp = sb("wg_tmp", [128, 8 * 16], F32)
        bg_sb = sb("bg_sb", [16, 1], F32)
        g_out = sb("g_out", [16, 512], F32)
        wgr = wgT.rearrange("(a p) n -> a p n", p=128)
        for a in range(8):
            s.dma(lambda e, a=a: e.dma_start(wg_tmp[:, a * 16:(a + 1) * 16], wgr[a]), "g")
        s.op("vector", lambda e: e.tensor_copy(wg_sb[:], wg_tmp[:]), "g")
        s.dma(lambda e: e.dma_start(bg_sb[:], bg), "g")
        for nch in range(4):
            for k in range(8):
                s.op("tensor", lambda e, k=k, nch=nch: e.matmul(
                    ps_a[0:16, 0:512],
                    wg_sb[:, k * 16:(k + 1) * 16],
                    xn2T_sb[:, k * TPC + nch * 512: k * TPC + (nch + 1) * 512],
                    start=(k == 0), stop=(k == 7)), "g", force_wait=(k == 0))
            s.op("scalar", lambda e: e.activation(g_out[:], ps_a[0:16, 0:512], AF.Sigmoid, bias=bg_sb[:]), "g")
            s.dma(lambda e, nch=nch: e.dma_start(gate[:, nch * 512:(nch + 1) * 512], g_out[:]), "g")
        s.emit(blk)
    return nc



def build_scan():
    NH = 4
    NC_ = 64
    nc = bass.Bass("TRN2", target_bir_lowering=False)
    qf = nc.dram_tensor("qf", [NH, 64, SQ], BF16, kind="ExternalInput").ap()
    kf = nc.dram_tensor("kf", [NH, 64, SQ], BF16, kind="ExternalInput").ap()
    vh = nc.dram_tensor("vh", [NH, SQ, 64], BF16, kind="ExternalInput").ap()
    gate = nc.dram_tensor("gate", [NH, SQ], F32, kind="ExternalInput").ap()
    tri = nc.dram_tensor("tri", [128, 128], F32, kind="ExternalInput").ap()
    sel = nc.dram_tensor("sel", [NH, 512], F32, kind="ExternalInput").ap()
    yh = nc.dram_tensor("yh", [NH, SQ, 64], BF16, kind="ExternalOutput").ap()

    with ExitStack() as ctx:
        def sb(n, shp, dt): return ctx.enter_context(nc.sbuf_tensor(n, shp, dt))
        def ps(n, shp, dt): return ctx.enter_context(nc.psum_tensor(n, shp, dt))
        chans = ["g"] + [f"h{i}" for i in range(4)]
        s = Seq(nc, ctx, chans=chans)
        qf2 = [sb(f"qf{i}", [128, SQ], BF16) for i in range(2)]
        kf2 = [sb(f"kf{i}", [128, SQ], BF16) for i in range(2)]
        v_sb = [sb(f"v{i}", [128, NC_ * 64], BF16) for i in range(NH)]
        L4 = sb("L4", [NH, SQ], F32)
        sel_sb = sb("sel_sb", [NH, 512], F32)
        tri_sb = sb("tri_sb", [128, 128], F32)
        one1 = sb("one1", [1, 1], F32)
        Lrel4 = sb("Lrel4", [NH, 128], F32)
        p4 = sb("p4", [NH, 128], F32)
        r24 = sb("r24", [NH, 128], F32)
        negL4 = sb("negL4", [NH, 128], F32)
        D_sb = [sb(f"D{i}", [128, 128], F32) for i in range(NH)]
        R_sb = [sb(f"R{i}", [128, 128], BF16) for i in range(NH)]
        A_sb = [sb(f"A{i}", [128, 128], BF16) for i in range(NH)]
        qt_sb = [sb(f"qt{i}", [64, 128], BF16) for i in range(NH)]
        kh_sb = [sb(f"kh{i}", [128, 64], BF16) for i in range(NH)]
        St_sb = [sb(f"St{i}", [64, 64], F32) for i in range(NH)]
        S_fp = [sb(f"Sfp{i}", [64, 64], F32) for i in range(NH)]
        S_bf = [sb(f"Sbf{i}", [64, 64], BF16) for i in range(NH)]
        y_out = [sb(f"yo{i}", [128, 64], BF16) for i in range(NH)]
        PA = [ps(f"PA{i}", [128, 512], F32) for i in range(NH)]
        PB = [ps(f"PB{i}", [128, 512], F32) for i in range(NH)]
        blk = ctx.enter_context(nc.Block())

        s.op("vector", lambda e: e.memset(one1[:], 1.0), "g")
        s.dma(lambda e: e.dma_start(tri_sb[:], tri), "g")
        s.dma(lambda e: e.dma_start(sel_sb[:], sel), "g")
        for i in range(NH):
            p, sub = i // 2, i % 2
            s.dma(lambda e, i=i, p=p, sub=sub: e.dma_start(
                qf2[p][sub * 64:(sub + 1) * 64, :], qf[i]), "g")
            s.dma(lambda e, i=i, p=p, sub=sub: e.dma_start(
                kf2[p][sub * 64:(sub + 1) * 64, :], kf[i]), "g")
            vsrc = vh[i].rearrange("(c p) n -> p c n", p=128)
            s.dma(lambda e, i=i, vsrc=vsrc: e.dma_start(
                v_sb[i][:].rearrange("p (c n) -> p c n", n=64), vsrc), "g")
            s.op("vector", lambda e, i=i: e.memset(S_fp[i][:], 0.0), "g")
            s.op("vector", lambda e, i=i: e.memset(S_bf[i][:], 0.0), "g")
        with nc.sbuf_tensor("g4", [NH, SQ], F32) as g4:
            s.dma(lambda e: e.dma_start(g4[:], gate), "g")
            s.op("scalar", lambda e: e.activation(g4[:], g4[:], AF.Ln), "g", force_wait=True)
            s.op("vector", lambda e: e.tensor_tensor_scan(
                L4[:], g4[:], g4[:], 0.0, ALU.add, ALU.bypass), "g", force_wait=True)
        kftm_sb = [sb(f"kftm{i}", [128, NC_ * 64], BF16) for i in range(NH)]
        for i in range(NH):
            for c in range(NC_):
                s.dma(lambda e, i=i, c=c: e.dma_start(
                    kftm_sb[i][:, c * 64:(c + 1) * 64],
                    kf[i][:, c * 128:(c + 1) * 128], transpose=True), "g")
        pre = s.mark("g")

        for c in range(NC_):
            c0 = c * 128
            if c == 0:
                s.op("vector", lambda e, c0=c0: e.tensor_scalar(
                    Lrel4[:], L4[:, c0:c0 + 128], 0.0, None, ALU.subtract), "g", after=[pre])
            else:
                s.op("vector", lambda e, c0=c0: e.tensor_scalar(
                    Lrel4[:], L4[:, c0:c0 + 128], L4[:, c0 - 1:c0], None, ALU.subtract), "g",
                    force_wait=True)
            s.op("scalar", lambda e: e.activation(p4[:], Lrel4[:], AF.Exp), "g", force_wait=True)
            s.op("scalar", lambda e, c0=c0: e.activation(
                r24[:], L4[:, c0:c0 + 128], AF.Exp, scale=-1.0,
                bias=L4[:, c0 + 127:c0 + 128]), "g")
            s.op("scalar", lambda e, c0=c0: e.mul(negL4[:], L4[:, c0:c0 + 128], -1.0), "g")
            rows_m = s.mark("g")
            for i in range(NH):
                ch = f"h{i}"
                p, sub = i // 2, i % 2
                seli = sel_sb[:, i * 128:(i + 1) * 128]
                s.op("tensor", lambda e, i=i, c0=c0, seli=seli: e.matmul(
                    PA[i][:, 0:128], seli, L4[:, c0:c0 + 128],
                    start=True, stop=False), ch, after=[rows_m])
                s.op("tensor", lambda e, i=i, seli=seli: e.matmul(
                    PA[i][:, 0:128], negL4[:], seli,
                    start=False, stop=True, skip_group_check=True), ch)
                s.op("vector", lambda e, i=i: e.tensor_tensor(
                    D_sb[i][:], PA[i][:, 0:128], tri_sb[:], ALU.add), ch)
                s.op("scalar", lambda e, i=i: e.activation(R_sb[i][:], D_sb[i][:], AF.Exp), ch)
                s.op("tensor", lambda e, i=i, c0=c0, p=p, sub=sub: e.matmul(
                    PA[i][:, 128:256], kf2[p][sub * 64:(sub + 1) * 64, c0:c0 + 128],
                    qf2[p][sub * 64:(sub + 1) * 64, c0:c0 + 128], start=True, stop=True), ch)
                s.op("vector", lambda e, i=i: e.tensor_tensor(
                    A_sb[i][:], PA[i][:, 128:256], R_sb[i][:], ALU.mult), ch)
                s.op("tensor", lambda e, i=i, seli=seli: e.matmul(
                    PA[i][0:64, 256:384], seli[:, 0:64], p4[:],
                    start=True, stop=True), ch)
                s.op("vector", lambda e, i=i, c0=c0, p=p, sub=sub: e.tensor_tensor(
                    qt_sb[i][:], qf2[p][sub * 64:(sub + 1) * 64, c0:c0 + 128],
                    PA[i][0:64, 256:384], ALU.mult), ch)
                s.op("tensor", lambda e, i=i, c=c: e.matmul(
                    PB[i][:, 0:64], A_sb[i][:], v_sb[i][:, c * 64:(c + 1) * 64],
                    start=True, stop=False), ch)
                s.op("tensor", lambda e, i=i: e.matmul(
                    PB[i][:, 0:64], qt_sb[i][:], S_bf[i][:],
                    start=False, stop=True, skip_group_check=True), ch)
                s.op("scalar", lambda e, i=i: e.copy(y_out[i][:], PB[i][:, 0:64]), ch)
                s.dma(lambda e, i=i, c0=c0: e.dma_start(yh[i][c0:c0 + 128, :], y_out[i][:]), ch)
                s.op("tensor", lambda e, i=i, seli=seli: e.matmul(
                    PB[i][:, 128:129], r24[:], seli[:, 0:1], start=True, stop=True), ch)
                s.op("vector", lambda e, i=i, c=c: e.tensor_scalar(
                    kh_sb[i][:], kftm_sb[i][:, c * 64:(c + 1) * 64], PB[i][:, 128:129], None,
                    ALU.mult), ch)
                s.op("tensor", lambda e, i=i, c=c: e.matmul(
                    PB[i][0:64, 64:128], kh_sb[i][:], v_sb[i][:, c * 64:(c + 1) * 64],
                    start=True, stop=True), ch)
                s.op("tensor", lambda e, i=i, seli=seli, c0=c0: e.matmul(
                    PB[i][0:64, 129:130], seli[:, 0:64], p4[:, 127:128], start=True, stop=True), ch)
                s.op("vector", lambda e, i=i: e.tensor_scalar(
                    St_sb[i][:], S_fp[i][:], PB[i][0:64, 129:130], None, ALU.mult), ch)
                s.op("vector", lambda e, i=i: e.tensor_tensor(
                    S_fp[i][:], St_sb[i][:], PB[i][0:64, 64:128], ALU.add), ch)
                s.op("scalar", lambda e, i=i: e.copy(S_bf[i][:], S_fp[i][:]), ch)
        s.emit(blk)
    return nc



def build_mix1():
    NEX = 128
    NG = 64
    nc = bass.Bass("TRN2", target_bir_lowering=False)
    out1 = nc.dram_tensor("out1", [TPC, D], F32, kind="ExternalInput").ap()
    y = nc.dram_tensor("y", [TPC, D], BF16, kind="ExternalInput").ap()
    woT = nc.dram_tensor("woT", [D, D], F32, kind="ExternalInput").ap()
    xn2tm = nc.dram_tensor("xn2tm", [TPC, D], BF16, kind="ExternalInput").ap()
    xn2T = nc.dram_tensor("xn2T", [D, TPC], BF16, kind="ExternalInput").ap()
    wc = nc.dram_tensor("wc", [D, NEX], F32, kind="ExternalInput").ap()
    bc = nc.dram_tensor("bc", [NEX, 1], F32, kind="ExternalInput").ap()
    tie = nc.dram_tensor("tie", [128, 512], F32, kind="ExternalInput").ap()
    w1 = nc.dram_tensor("w1", [NEX, D, 32], F32, kind="ExternalInput").ap()
    w2t = nc.dram_tensor("w2t", [NEX, 32, D], F32, kind="ExternalInput").ap()
    b1 = nc.dram_tensor("b1", [NEX, 32], F32, kind="ExternalInput").ap()
    b2 = nc.dram_tensor("b2", [NEX, D], F32, kind="ExternalInput").ap()
    ident = nc.dram_tensor("ident", [128, 128], BF16, kind="ExternalInput").ap()
    out = nc.dram_tensor("out", [TPC, D], F32, kind="ExternalOutput").ap()
    o1b_hbm = nc.dram_tensor("o1b_scr", [TPC, D], F32, kind="Internal").ap()
    rp_hbm = nc.dram_tensor("rp_scr", [NEX, NG, D], BF16, kind="Internal").ap()

    with ExitStack() as ctx:
        def sb(n, shp, dt): return ctx.enter_context(nc.sbuf_tensor(n, shp, dt))
        def ps(n, shp, dt): return ctx.enter_context(nc.psum_tensor(n, shp, dt))
        s = Seq(nc, ctx, chans=("g", "e0", "e1"))
        ident_sb = sb("ident_sb", [128, 128], BF16)
        blk = ctx.enter_context(nc.Block())

        o1r = out1.rearrange("(t p) n -> t p n", p=128)
        o1br = o1b_hbm.rearrange("(t p) n -> t p n", p=128)
        with nc.sbuf_tensor("yT_sb", [128, 8 * TPC], BF16) as yT_sb, \
             nc.sbuf_tensor("wa", [128, 8 * D], BF16) as wa, \
             nc.sbuf_tensor("wtmp", [128, 8 * D], F32) as wtmp, \
             nc.sbuf_tensor("xt", [128, D], F32) as xt, \
             nc.sbuf_tensor("o1b_sb", [128, D], F32) as o1b_sb, \
             nc.psum_tensor("ps_a", [128, 1024], F32) as ps_a:
            for j in range(8):
                s.dma(lambda e, j=j: e.dma_start(
                    yT_sb[:, j * TPC:(j + 1) * TPC], y[:, j * 128:(j + 1) * 128], transpose=True), "g")
            load_weight_bf(s, woT, wa, wtmp, 8, D, chan="g")
            s.dma(lambda e: e.dma_start(ident_sb[:], ident), "g")
            for tt in range(NTT):
                s.dma(lambda e, tt=tt: e.dma_start(xt[:], o1r[tt]), "g")
                for nch in range(2):
                    for k in range(8):
                        s.op("tensor", lambda e, k=k, nch=nch, tt=tt: e.matmul(
                            ps_a[:, nch * 512:(nch + 1) * 512],
                            yT_sb[:, k * TPC + tt * 128: k * TPC + (tt + 1) * 128],
                            wa[:, k * D + nch * 512: k * D + (nch + 1) * 512],
                            start=(k == 0), stop=(k == 7)), "g", force_wait=(k == 0 and nch == 0))
                s.op("vector", lambda e: e.tensor_tensor(o1b_sb[:], xt[:], ps_a[:], ALU.add), "g",
                     force_wait=True)
                s.dma(lambda e, tt=tt: e.dma_start(o1br[tt], o1b_sb[:]), "g")
        stageA = s.mark("g")

        xn2tm_sb = sb("xn2tm_sb", [128, NTT * D], BF16)
        wc_sb = sb("wc_sb", [128, 8 * NEX], BF16)
        wc_tmp = sb("wc_tmp", [128, 8 * NEX], F32)
        bc_sb = sb("bc_sb", [128, 1], F32)
        tie_sb = sb("tie_sb", [128, 512], F32)
        lg_sb = sb("lg_sb", [128, TPC], F32)
        mx_sb = sb("mx_sb", [128, 1], F32)
        perm_sb = sb("perm_sb", [128, TPC], BF16)
        permT_blk = sb("permT_blk", [128, NTT * 512], BF16)
        perm_x = sb("perm_x", [128, 8 * 64 * 64], BF16)
        xr2 = xn2tm.rearrange("(t p) n -> t p n", p=128)
        xn2Tr = xn2T.rearrange("(m p) n -> m p n", p=128)
        for tt in range(NTT):
            s.dma(lambda e, tt=tt: e.dma_start(xn2tm_sb[:, tt * D:(tt + 1) * D], xr2[tt]), "g")
        wcr = wc.rearrange("(a p) n -> a p n", p=128)
        for a in range(8):
            s.dma(lambda e, a=a: e.dma_start(wc_tmp[:, a * NEX:(a + 1) * NEX], wcr[a]), "g")
        s.op("vector", lambda e: e.tensor_copy(wc_sb[:], wc_tmp[:]), "g")
        s.dma(lambda e: e.dma_start(bc_sb[:], bc), "g")
        s.dma(lambda e: e.dma_start(tie_sb[:], tie), "g")

        with nc.psum_tensor("ps_l", [128, 512], F32) as ps_l, \
             nc.psum_tensor("ps_pt", [32, 128], BF16) as ps_pt, \
             nc.sbuf_tensor("xn2T_sb", [128, 8 * TPC], BF16) as xn2T_sb:
            for j in range(8):
                s.dma(lambda e, j=j: e.dma_start(xn2T_sb[:, j * TPC:(j + 1) * TPC], xn2Tr[j]), "g")
            for nch in range(4):
                for k in range(8):
                    s.op("tensor", lambda e, k=k, nch=nch: e.matmul(
                        ps_l[:], wc_sb[:, k * NEX:(k + 1) * NEX],
                        xn2T_sb[:, k * TPC + nch * 512: k * TPC + (nch + 1) * 512],
                        start=(k == 0), stop=(k == 7)), "g", force_wait=(k == 0))
                s.op("vector", lambda e, nch=nch: e.tensor_scalar(
                    lg_sb[:, nch * 512:(nch + 1) * 512], ps_l[:], bc_sb[:], None, ALU.add), "g",
                    force_wait=True)
                s.op("vector", lambda e, nch=nch: e.tensor_tensor(
                    lg_sb[:, nch * 512:(nch + 1) * 512], lg_sb[:, nch * 512:(nch + 1) * 512],
                    tie_sb[:], ALU.add), "g")
            s.op("vector", lambda e: e.memset(permT_blk[:], 0.0), "g")
            for g in range(NG):
                s.op("vector", lambda e, g=g: e.reduce_max(
                    mx_sb[:], lg_sb[:, g * 32:(g + 1) * 32], axis=AX.X), "g")
                s.op("vector", lambda e, g=g: e.tensor_scalar(
                    perm_sb[:, g * 32:(g + 1) * 32], lg_sb[:, g * 32:(g + 1) * 32],
                    mx_sb[:], None, ALU.is_equal), "g", force_wait=True)
                s.op("tensor", lambda e, g=g: e.transpose(
                    ps_pt[:], perm_sb[:, g * 32:(g + 1) * 32], ident_sb[:]), "g", force_wait=True)
                gl, tt = g % 4, g // 4
                s.op("scalar", lambda e, gl=gl, tt=tt: e.copy(
                    permT_blk[gl * 32:(gl + 1) * 32, tt * 512 + gl * 128: tt * 512 + (gl + 1) * 128],
                    ps_pt[:]), "g", force_wait=True)
        permdone = s.mark("g")

        ptv = permT_blk[:].rearrange("p (t g e) -> p t g e", t=NTT, g=4, e=128)
        pxv = perm_x[:].rearrange("p (d g e) -> p d g e", d=8, g=64, e=64)
        ones64_bf = sb("ones64_bf", [1, 64], BF16)
        s.op("vector", lambda e: e.memset(ones64_bf[:], 1.0), "g")
        w1_bf = [sb(f"w1bf{i}", [128, 256], BF16) for i in range(2)]
        w2t_bf = [sb(f"w2tbf{i}", [32, D], BF16) for i in range(2)]
        b1_bf = [sb(f"b1bf{i}", [1, 32], BF16) for i in range(2)]
        b2_bf = [sb(f"b2bf{i}", [1, D], BF16) for i in range(2)]
        wtmp1 = [sb(f"wtmp1_{i}", [128, 256], F32) for i in range(2)]
        wtmp2 = [sb(f"wtmp2_{i}", [32, D], F32) for i in range(2)]
        btmp1 = [sb(f"btmp1_{i}", [1, 32], F32) for i in range(2)]
        btmp2 = [sb(f"btmp2_{i}", [1, D], F32) for i in range(2)]
        inner_bf = [sb(f"inner{i}", [32, 64], BF16) for i in range(2)]
        rp_bf = [sb(f"rp{i}", [64, D], BF16) for i in range(2)]

        with nc.psum_tensor("gps0", [128, 256], F32) as gps0, \
             nc.psum_tensor("gps1", [128, 256], F32) as gps1, \
             nc.psum_tensor("ip0", [32, 64], F32) as ip0, \
             nc.psum_tensor("ip1", [32, 64], F32) as ip1, \
             nc.psum_tensor("rpp0", [64, 1024], F32) as rpp0, \
             nc.psum_tensor("rpp1", [64, 1024], F32) as rpp1:
            gps = [gps0, gps1]
            ipp = [ip0, ip1]
            rpp = [rpp0, rpp1]
            for H in range(2):
                idx = 0
                for tt in range(NTT):
                    for dt in range(8):
                        p = gps[idx % 2]
                        ch = "g"
                        s.op("tensor", lambda e, p=p, tt=tt, dt=dt, H=H: e.matmul(
                            p[:], xn2tm_sb[:, tt * D + dt * 128: tt * D + (dt + 1) * 128],
                            ptv[:, tt, :, H * 64:(H + 1) * 64], start=True, stop=True),
                            ch, force_wait=True, after=[permdone] if idx == 0 else ())
                        s.op("scalar", lambda e, p=p, tt=tt, dt=dt: e.copy(
                            perm_x[:, dt * 4096 + tt * 256: dt * 4096 + (tt + 1) * 256], p[:]),
                            ch, force_wait=True)
                        idx += 1
                gath = s.mark("g")
                for el in range(64):
                    es = H * 64 + el
                    ch = f"e{el % 2}"
                    bi = el % 2
                    w1r = w1[es].rearrange("(a p) f -> a p f", p=128)
                    for a in range(8):
                        s.dma(lambda e, a=a, bi=bi, w1r=w1r: e.dma_start(
                            wtmp1[bi][:, a * 32:(a + 1) * 32], w1r[a]), ch,
                            after=[gath] if a == 0 else ())
                    s.op("vector", lambda e, bi=bi: e.tensor_copy(w1_bf[bi][:], wtmp1[bi][:]), ch)
                    s.dma(lambda e, es=es, bi=bi: e.dma_start(wtmp2[bi][:], w2t[es]), ch)
                    s.op("vector", lambda e, bi=bi: e.tensor_copy(w2t_bf[bi][:], wtmp2[bi][:]), ch)
                    s.dma(lambda e, es=es, bi=bi: e.dma_start(btmp1[bi][:], b1[es:es + 1, :]), ch)
                    s.op("vector", lambda e, bi=bi: e.tensor_copy(b1_bf[bi][:], btmp1[bi][:]), ch)
                    s.dma(lambda e, es=es, bi=bi: e.dma_start(btmp2[bi][:], b2[es:es + 1, :]), ch)
                    s.op("vector", lambda e, bi=bi: e.tensor_copy(b2_bf[bi][:], btmp2[bi][:]), ch)
                    for a in range(8):
                        s.op("tensor", lambda e, a=a, bi=bi, el=el: e.matmul(
                            ipp[bi][:], w1_bf[bi][:, a * 32:(a + 1) * 32],
                            pxv[:, a, :, el], start=(a == 0), stop=False), ch,
                            force_wait=(a == 0))
                    s.op("tensor", lambda e, bi=bi: e.matmul(
                        ipp[bi][:], b1_bf[bi][:], ones64_bf[:],
                        start=False, stop=True, skip_group_check=True), ch)
                    s.op("scalar", lambda e, bi=bi: e.activation(inner_bf[bi][:], ipp[bi][:], AF.Relu),
                         ch, force_wait=True)
                    for nch in range(2):
                        s.op("tensor", lambda e, bi=bi, nch=nch: e.matmul(
                            rpp[bi][:, nch * 512:(nch + 1) * 512], inner_bf[bi][:],
                            w2t_bf[bi][:, nch * 512:(nch + 1) * 512], start=True, stop=False),
                            ch, force_wait=(nch == 0))
                        s.op("tensor", lambda e, bi=bi, nch=nch: e.matmul(
                            rpp[bi][:, nch * 512:(nch + 1) * 512], ones64_bf[:],
                            b2_bf[bi][0:1, nch * 512:(nch + 1) * 512],
                            start=False, stop=True, skip_group_check=True), ch)
                    s.op("scalar", lambda e, bi=bi: e.copy(rp_bf[bi][:], rpp[bi][:]), ch,
                         force_wait=True)
                    s.dma(lambda e, es=es, bi=bi: e.dma_start(rp_hbm[es], rp_bf[bi][:]), ch)
        eAm, eBm = s.mark("e0"), s.mark("e1")

        rg = [sb(f"rg{i}", [128, D], BF16) for i in range(2)]
        o32 = sb("o32", [32, D], F32)
        f32_sb = sb("f32_sb", [32, D], F32)
        rpg = rp_hbm.rearrange("e g d -> g e d")
        outr = out.rearrange("(g p) n -> g p n", p=32)
        o1bg = o1b_hbm.rearrange("(g p) n -> g p n", p=32)
        with nc.psum_tensor("sc_ps", [32, 1024], F32) as sc_ps:
            for g in range(NG):
                bi = g % 2
                s.dma(lambda e, g=g, bi=bi: e.dma_start(rg[bi][:], rpg[g]), "g",
                      after=[eAm, eBm] if g == 0 else ())
                s.dma(lambda e, g=g: e.dma_start(o32[:], o1bg[g]), "g")
                for nch in range(2):
                    s.op("tensor", lambda e, g=g, bi=bi, nch=nch: e.matmul(
                        sc_ps[:, nch * 512:(nch + 1) * 512], perm_sb[:, g * 32:(g + 1) * 32],
                        rg[bi][:, nch * 512:(nch + 1) * 512], start=True, stop=True), "g",
                        force_wait=(nch == 0))
                s.op("vector", lambda e: e.tensor_tensor(f32_sb[:], o32[:], sc_ps[:], ALU.add), "g",
                     force_wait=True)
                s.dma(lambda e, g=g: e.dma_start(outr[g], f32_sb[:]), "g")
        s.emit(blk)
    return nc



_cache = {}


def _get(name, builder):
    if name not in _cache:
        _cache[name] = builder()
    return _cache[name]


def kernel(L0_attn_x, L0_attn_weight_qkv, L0_attn_out_w, L0_attn_out_b, L0_ffn_weights,
           L0_ffn_scale, L0_ffn_bias, L1_attn_Wq, L1_attn_Wk, L1_attn_Wv, L1_attn_Wo,
           L1_attn_Wg, L1_attn_bg, L1_ffn_Wc, L1_ffn_bc, L1_ffn_W1, L1_ffn_b1,
           L1_ffn_W2, L1_ffn_b2):
    cores = list(range(NCORE))
    x = np.ascontiguousarray(np.asarray(L0_attn_x, dtype=np.float32))
    C = np.ascontiguousarray

    def slc(c):
        b, p = divmod(c, 4)
        return b, p * TPC

    nc1 = _get("qkv0", build_qkv0)
    wqkv = C(np.asarray(L0_attn_weight_qkv, dtype=np.float32))
    in1 = []
    for c in cores:
        b, r0 = slc(c)
        in1.append({"x": C(x[b, r0:r0 + TPC]), "wqkv": wqkv})
    r1 = run_bass_kernel_spmd(nc1, in1, core_ids=cores).results

    kT_full = [np.concatenate([r1[b * 4 + p]["kT"] for p in range(4)], axis=1) for b in range(2)]
    v_full = [np.concatenate([r1[b * 4 + p]["v"] for p in range(4)], axis=0) for b in range(2)]

    nc2 = _get("attn0", build_attn0)
    in2 = []
    for c in cores:
        b, r0 = slc(c)
        in2.append({"qT": r1[c]["qT"], "kT": C(kT_full[b]), "v": C(v_full[b])})
    r2 = run_bass_kernel_spmd(nc2, in2, core_ids=cores).results

    nc3 = _get("mix0", build_mix0)
    in3 = []
    owT = C(np.asarray(L0_attn_out_w, dtype=np.float32).T)
    outb = C(np.asarray(L0_attn_out_b, dtype=np.float32)[None, :])
    fwT = C(np.asarray(L0_ffn_weights, dtype=np.float32).T)
    fscale = C(np.asarray(L0_ffn_scale, dtype=np.float32)[None, :])
    fbias = C(np.asarray(L0_ffn_bias, dtype=np.float32)[None, :])
    wqT = C(np.asarray(L1_attn_Wq, dtype=np.float32).T)
    wkT = C(np.asarray(L1_attn_Wk, dtype=np.float32).T)
    wvT = C(np.asarray(L1_attn_Wv, dtype=np.float32).T)
    wgT = C(np.asarray(L1_attn_Wg, dtype=np.float32).T)
    bgc = C(np.asarray(L1_attn_bg, dtype=np.float32)[:, None])
    for c in cores:
        b, r0 = slc(c)
        in3.append({"x": C(x[b, r0:r0 + TPC]), "yT": r2[c]["yT"], "xnT": r1[c]["xnT"],
                    "owT": owT, "outb": outb, "fwT": fwT, "fscale": fscale, "fbias": fbias,
                    "wqT": wqT, "wkT": wkT, "wvT": wvT, "wgT": wgT, "bg": bgc})
    r3 = run_bass_kernel_spmd(nc3, in3, core_ids=cores).results

    qfT_full = [np.concatenate([r3[b * 4 + p]["qfT"] for p in range(4)], axis=1) for b in range(2)]
    kfT_full = [np.concatenate([r3[b * 4 + p]["kfT"] for p in range(4)], axis=1) for b in range(2)]
    v1_full = [np.concatenate([r3[b * 4 + p]["v1"] for p in range(4)], axis=0) for b in range(2)]
    gate_full = [np.concatenate([r3[b * 4 + p]["gate"] for p in range(4)], axis=1) for b in range(2)]

    nc4 = _get("scan", build_scan)
    tri = np.where(np.arange(128)[:, None] <= np.arange(128)[None, :], 0.0, -1e9).astype(np.float32)
    sel = np.zeros((4, 512), dtype=np.float32)
    for i in range(4):
        sel[i, i * 128:(i + 1) * 128] = 1.0
    in4 = []
    for c in cores:
        idxs = [4 * c + j for j in range(4)]
        qf = np.stack([qfT_full[i // 16][(i % 16) * 64:(i % 16 + 1) * 64] for i in idxs])
        kf = np.stack([kfT_full[i // 16][(i % 16) * 64:(i % 16 + 1) * 64] for i in idxs])
        vv = np.stack([v1_full[i // 16][:, (i % 16) * 64:(i % 16 + 1) * 64] for i in idxs])
        gg = np.stack([gate_full[i // 16][i % 16] for i in idxs])
        in4.append({"qf": C(qf), "kf": C(kf), "vh": C(vv), "gate": C(gg), "tri": tri, "sel": sel})
    r4 = run_bass_kernel_spmd(nc4, in4, core_ids=cores).results

    y_tm = [np.zeros((SQ, D), dtype=BF) for _ in range(2)]
    for c in cores:
        for j in range(4):
            i = 4 * c + j
            y_tm[i // 16][:, (i % 16) * 64:(i % 16 + 1) * 64] = r4[c]["yh"][j]

    nc5 = _get("mix1", build_mix1)
    woT = C(np.asarray(L1_attn_Wo, dtype=np.float32).T)
    wcf = C(np.asarray(L1_ffn_Wc, dtype=np.float32).reshape(D, 128))
    bcf = C(np.asarray(L1_ffn_bc, dtype=np.float32).reshape(128, 1))
    tie_pat = np.linspace(0.0, 1e-6, 32, dtype=np.float32)
    tie_mat = C(np.broadcast_to(np.tile(tie_pat, 16)[None, :], (128, 512)).copy())
    w1f = C(np.asarray(L1_ffn_W1, dtype=np.float32).reshape(128, D, 32))
    w2tf = C(np.asarray(L1_ffn_W2, dtype=np.float32).transpose(0, 1, 3, 2).reshape(128, 32, D))
    b1f = C(np.asarray(L1_ffn_b1, dtype=np.float32).reshape(128, 32))
    b2f = C(np.asarray(L1_ffn_b2, dtype=np.float32).reshape(128, D))
    identbf = np.eye(128, dtype=BF)
    in5 = []
    for c in cores:
        b, r0 = slc(c)
        in5.append({"out1": r3[c]["out1"], "y": C(y_tm[b][r0:r0 + TPC]), "woT": woT,
                    "xn2tm": r3[c]["xn2tm"], "xn2T": r3[c]["xn2T"], "wc": wcf, "bc": bcf,
                    "tie": tie_mat, "w1": w1f, "w2t": w2tf, "b1": b1f, "b2": b2f,
                    "ident": identbf})
    r5 = run_bass_kernel_spmd(nc5, in5, core_ids=cores).results

    outp = np.zeros((2, SQ, D), dtype=np.float32)
    for c in cores:
        b, r0 = slc(c)
        outp[b, r0:r0 + TPC] = r5[c]["out"]
    return outp


# version 02 (final)
# speedup vs baseline: 1.0397x; 1.0397x over previous
import numpy as np
import ml_dtypes
from contextlib import ExitStack
import concourse.bass as bass
import concourse.mybir as mybir
from concourse.bass_utils import run_bass_kernel_spmd

F32 = mybir.dt.float32
BF16 = mybir.dt.bfloat16
AF = mybir.ActivationFunctionType
ALU = mybir.AluOpType
AX = mybir.AxisListType
BF = ml_dtypes.bfloat16

D = 1024
NCORE = 8
TPC = 2048
NTT = TPC // 128
SQ = 8192


class Seq:

    def __init__(self, nc, ctx, chans):
        self.nc = nc
        self.sems = {c: ctx.enter_context(nc.semaphore(f"sem_{c}")) for c in chans}
        self.count = {c: 0 for c in chans}
        self.other_count = {c: {} for c in chans}
        self.last_eng = {c: None for c in chans}
        self.ops = []

    def op(self, eng, fn, chan="g", inc=None, after=(), force_wait=False):
        if inc is None:
            inc = 16 if eng in ("sync",) or getattr(fn, "_is_dma", False) else 1
        waits = []
        tgt = self.count[chan] if force_wait else self.other_count[chan].get(eng, 0)
        if tgt > 0:
            waits.append((chan, tgt))
        for m in after:
            if m is not None:
                waits.append(m)
        self.ops.append((eng, fn, chan, waits, inc))
        newc = self.count[chan] + inc
        for e2 in ("sync", "tensor", "vector", "scalar", "gpsimd"):
            if e2 != eng:
                self.other_count[chan][e2] = newc
        self.count[chan] = newc
        self.last_eng[chan] = eng

    def dma(self, fn, chan="g", after=(), eng="sync"):
        self.op(eng, fn, chan, inc=16, after=after)

    def mark(self, chan):
        return (chan, self.count[chan])

    def barrier_marks(self):
        return [self.mark(c) for c in self.count]

    def emit(self, blk):
        final = dict(self.count)

        def mk(name):
            def body(eng):
                lastw = {}
                for (e, fn, chan, waits, inc) in self.ops:
                    if e != name:
                        continue
                    for (c, v) in waits:
                        if v > 0 and lastw.get(c, -1) < v:
                            eng.wait_ge(self.sems[c], v)
                            lastw[c] = v
                    fn(eng).then_inc(self.sems[chan], inc)
                if name == "sync":
                    for c, v in final.items():
                        if v > 0:
                            eng.wait_ge(self.sems[c], v)
            return body

        for name in ("sync", "tensor", "vector", "scalar", "gpsimd"):
            blk.__getattr__(name)(mk(name)) if False else getattr(blk, name)(mk(name))


def ln_tile(s, x_tile, xn_bf_tile, st6, st2, tmp1, eps_col, chan="g"):
    s.op("vector", lambda e: e.bn_stats(st6[:, 0:6], x_tile[:, 0:512]), chan)
    s.op("vector", lambda e: e.bn_stats(st6[:, 6:12], x_tile[:, 512:1024]), chan)
    s.op("vector", lambda e: e.bn_aggr(st2[:], st6[:]), chan, force_wait=True)
    s.op("scalar", lambda e: e.activation(tmp1[:], st2[:, 1:2], AF.Sqrt, bias=eps_col[:]), chan, force_wait=True)
    s.op("vector", lambda e: e.reciprocal(tmp1[:], tmp1[:]), chan)
    s.op("vector", lambda e: e.tensor_scalar(xn_bf_tile[:], x_tile[:], st2[:, 0:1], tmp1[:],
                                             ALU.subtract, ALU.mult), chan, force_wait=True)


def load_weight_bf(s, hbm_ap, wsb, wtmp, kt, cols, chan="g"):
    src = hbm_ap.rearrange("(a p) n -> a p n", p=128)
    for a in range(kt):
        s.dma(lambda e, a=a: e.dma_start(wtmp[:, a * cols:(a + 1) * cols], src[a]), chan)
    s.op("vector", lambda e: e.tensor_copy(wsb[:], wtmp[:]), chan)



def build_qkv0():
    nc = bass.Bass("TRN2", target_bir_lowering=False)
    x = nc.dram_tensor("x", [TPC, D], F32, kind="ExternalInput").ap()
    wqkv = nc.dram_tensor("wqkv", [D, 3 * D], F32, kind="ExternalInput").ap()
    qT = nc.dram_tensor("qT", [D, TPC], BF16, kind="ExternalOutput").ap()
    kT = nc.dram_tensor("kT", [D, TPC], BF16, kind="ExternalOutput").ap()
    v = nc.dram_tensor("v", [TPC, D], BF16, kind="ExternalOutput").ap()
    xnT = nc.dram_tensor("xnT", [D, TPC], BF16, kind="ExternalOutput").ap()

    with ExitStack() as ctx:
        def sb(n, shp, dt): return ctx.enter_context(nc.sbuf_tensor(n, shp, dt))
        def ps(n, shp, dt): return ctx.enter_context(nc.psum_tensor(n, shp, dt))
        s = Seq(nc, ctx, chans=("g", "w"))
        xt = sb("xt", [128, D], F32)
        st6 = sb("st6", [128, 12], F32)
        st2 = sb("st2", [128, 2], F32)
        tmp1 = sb("tmp1", [128, 1], F32)
        eps_col = sb("eps_col", [128, 1], F32)
        xn_tm = sb("xn_tm", [128, D], BF16)
        xnT_sb = sb("xnT_sb", [128, 8 * TPC], BF16)
        wsb = sb("wsb", [128, 8 * 3 * D], BF16)
        wtmp = sb("wtmp", [128, 8 * 3 * D], F32)
        out_bf = sb("out_bf", [128, 2 * 512], BF16)
        pst = [ps(f"ps{i}", [128, 512], F32) for i in range(2)]
        blk = ctx.enter_context(nc.Block())

        s.op("vector", lambda e: e.memset(eps_col[:], 1e-5), "g")
        load_weight_bf(s, wqkv, wsb, wtmp, 8, 3 * D, chan="w")
        wdone = s.mark("w")

        xr = x.rearrange("(t p) n -> t p n", p=128)
        for tt in range(NTT):
            s.dma(lambda e, tt=tt: e.dma_start(xt[:], xr[tt]), "g")
            ln_tile(s, xt, xn_tm, st6, st2, tmp1, eps_col, "g")
            for j in range(8):
                s.dma(lambda e, tt=tt, j=j: e.dma_start(
                    xnT_sb[:, j * TPC + tt * 128: j * TPC + (tt + 1) * 128],
                    xn_tm[:, j * 128:(j + 1) * 128], transpose=True), "g")
        xndone = s.mark("g")

        def proj(out_hbm, col0, scale, n_m, mode):
            outr = out_hbm.rearrange("(m p) n -> m p n", p=128)
            idx = 0
            for m in range(n_m):
                nchunks = 4 if mode != "v" else 2
                for nch in range(nchunks):
                    p = pst[idx % 2]
                    idx += 1
                    for k in range(8):
                        if mode == "v":
                            lhs = xnT_sb[:, k * TPC + m * 128:(k * TPC) + (m + 1) * 128]
                            rhs = wsb[:, k * 3 * D + 2 * D + nch * 512: k * 3 * D + 2 * D + (nch + 1) * 512]
                        else:
                            lhs = wsb[:, k * 3 * D + col0 + m * 128: k * 3 * D + col0 + (m + 1) * 128]
                            rhs = xnT_sb[:, k * TPC + nch * 512: k * TPC + (nch + 1) * 512]
                        s.op("tensor", lambda e, p=p, lhs=lhs, rhs=rhs, k=k: e.matmul(
                            p[:], lhs, rhs, start=(k == 0), stop=(k == 7)), "g",
                            after=[wdone, xndone] if k == 0 else ())
                    ob = out_bf[:, (idx % 2) * 512:(idx % 2) * 512 + 512]
                    if scale is None:
                        s.op("scalar", lambda e, p=p, ob=ob: e.copy(ob, p[:]), "g")
                    else:
                        s.op("scalar", lambda e, p=p, ob=ob: e.mul(ob, p[:], scale), "g")
                    s.dma(lambda e, m=m, nch=nch, ob=ob, outr=outr: e.dma_start(
                        outr[m][:, nch * 512:(nch + 1) * 512], ob), "g")

        proj(qT, 0, 1.0 / 32.0, 8, "qk")
        proj(kT, D, None, 8, "qk")
        proj(v, None, None, NTT, "v")

        xnTr = xnT.rearrange("(m p) n -> m p n", p=128)
        for j in range(8):
            s.dma(lambda e, j=j: e.dma_start(xnTr[j], xnT_sb[:, j * TPC:(j + 1) * TPC]), "g")
        s.emit(blk)
    return nc



def build_attn0():
    nc = bass.Bass("TRN2", target_bir_lowering=False)
    qT = nc.dram_tensor("qT", [D, TPC], BF16, kind="ExternalInput").ap()
    kT = nc.dram_tensor("kT", [D, SQ], BF16, kind="ExternalInput").ap()
    v = nc.dram_tensor("v", [SQ, D], BF16, kind="ExternalInput").ap()
    yT = nc.dram_tensor("yT", [D, TPC], BF16, kind="ExternalOutput").ap()

    NQC = TPC // 256
    NKC = SQ // 1024
    NT = NKC * 8
    with ExitStack() as ctx:
        def sb(n, shp, dt): return ctx.enter_context(nc.sbuf_tensor(n, shp, dt))
        def ps(n, shp, dt): return ctx.enter_context(nc.psum_tensor(n, shp, dt))
        s = Seq(nc, ctx, chans=("pe", "ex", "ev", "kv", "pre"))
        qT_sb = sb("qT_sb", [128, 8 * TPC], BF16)
        kch = [sb(f"kch{i}", [128, 8 * 1024], BF16) for i in range(2)]
        vch = [sb(f"vch{i}", [128, 8 * 1024], BF16) for i in range(2)]
        E_sb = [sb(f"E{i}", [128, 256], BF16) for i in range(2)]
        ones_col = sb("ones_col", [128, 1], BF16)
        den_sb = sb("den_sb", [1, 256], F32)
        bc_sb = sb("bc_sb", [128, 256], F32)
        ones_row = sb("ones_row", [1, 128], F32)
        yT_sb = sb("yT_sb", [128, 8 * TPC], BF16)
        y_ps = ps("y_ps", [128, 2048], F32)
        s_ps = [ps(f"s_ps{i}", [128, 256], F32) for i in range(2)]
        den_ps = ps("den_ps", [1, 256], F32)
        bc_ps = ps("bc_ps", [128, 256], F32)
        blk = ctx.enter_context(nc.Block())

        qTr = qT.rearrange("(m p) n -> m p n", p=128)
        kTr = kT.rearrange("(m p) n -> m p n", p=128)
        vr = v.rearrange("(c t p) n -> c t p n", p=128, t=8)
        yTr = yT.rearrange("(m p) n -> m p n", p=128)

        s.op("vector", lambda e: e.memset(ones_col[:], 1.0), "pre")
        s.op("vector", lambda e: e.memset(ones_row[:], 1.0), "pre")
        for j in range(8):
            s.dma(lambda e, j=j: e.dma_start(qT_sb[:, j * TPC:(j + 1) * TPC], qTr[j]), "pre")
        pre_done = s.mark("pre")

        kv_loaded = {}
        pe_chunk_done = {}
        exp_done = {}
        y_done = {}
        ev_done = {}

        def load_chunk(gi):
            qc, c = divmod(gi, NKC)
            buf = gi % 2
            after = [pre_done]
            if gi >= 2:
                after.append(pe_chunk_done[gi - 2])
            for j in range(8):
                s.dma(lambda e, j=j, c=c, buf=buf: e.dma_start(
                    kch[buf][:, j * 1024:(j + 1) * 1024],
                    kTr[j][:, c * 1024:(c + 1) * 1024]), "kv", after=after if j == 0 else ())
            for t in range(8):
                s.dma(lambda e, t=t, c=c, buf=buf: e.dma_start(
                    vch[buf][:, t * 1024:(t + 1) * 1024], vr[c, t]), "kv")
            kv_loaded[gi] = s.mark("kv")

        def scores(ti):
            qc, t = divmod(ti, NT)
            gi = qc * NKC + t // 8
            buf = (t // 8) % 2
            tl = t % 8
            after = [kv_loaded[gi]]
            if ti >= 2:
                after.append(exp_done[ti - 2])
            if t == 0 and qc > 0:
                after.append(ev_done[qc - 1])
            for k in range(8):
                s.op("tensor", lambda e, k=k, buf=buf, tl=tl, qc=qc, t=t: e.matmul(
                    s_ps[t % 2][:],
                    kch[buf][:, k * 1024 + tl * 128: k * 1024 + (tl + 1) * 128],
                    qT_sb[:, k * TPC + (qc % 8) * 256: k * TPC + (qc % 8) * 256 + 256],
                    start=(k == 0), stop=(k == 7)), "pe", after=after if k == 0 else ())

        def expf(ti):
            qc, t = divmod(ti, NT)
            after = [s.mark("pe_scores_%d" % ti) if False else None]
            after = [scores_done[ti]]
            if ti >= 2:
                after.append(y_done[ti - 2])
            s.op("scalar", lambda e, t=t: e.activation(E_sb[t % 2][:], s_ps[t % 2][:], AF.Exp),
                 "ex", after=after)

        def ymm(ti):
            qc, t = divmod(ti, NT)
            gi = qc * NKC + t // 8
            buf = (t // 8) % 2
            tl = t % 8
            after = [exp_done[ti]]
            for dv in range(8):
                s.op("tensor", lambda e, dv=dv, buf=buf, tl=tl, t=t: e.matmul(
                    y_ps[:, dv * 256:(dv + 1) * 256],
                    vch[buf][:, tl * 1024 + dv * 128: tl * 1024 + (dv + 1) * 128],
                    E_sb[t % 2][:],
                    start=(t == 0 and dv % 2 == 0), stop=(t == NT - 1), skip_group_check=True),
                    "pe", after=after if dv == 0 else ())
            s.op("tensor", lambda e, t=t: e.matmul(
                den_ps[:], ones_col[:], E_sb[t % 2][:],
                start=(t == 0), stop=(t == NT - 1), skip_group_check=True), "pe")

        scores_done = {}
        for qc in range(NQC):
            for ti in range(qc * NT, (qc + 1) * NT):
                t = ti % NT
                if t % 8 == 0:
                    gi = qc * NKC + t // 8
                    load_chunk(gi)
                scores(ti)
                scores_done[ti] = s.mark("pe")
                expf(ti)
                exp_done[ti] = s.mark("ex")
                ymm(ti)
                y_done[ti] = s.mark("pe")
                if t % 8 == 7:
                    pe_chunk_done[qc * NKC + t // 8] = s.mark("pe")
            s.op("vector", lambda e: e.reciprocal(den_sb[:], den_ps[:]), "ev",
                 after=[y_done[(qc + 1) * NT - 1]])
            rec_m = s.mark("ev")
            s.op("tensor", lambda e: e.matmul(bc_ps[:], ones_row[:], den_sb[:],
                                              start=True, stop=True), "pe", after=[rec_m])
            bc_m = s.mark("pe")
            s.op("scalar", lambda e: e.copy(bc_sb[:], bc_ps[:]), "ev", after=[bc_m])
            for dv in range(8):
                s.op("vector", lambda e, dv=dv, qc=qc: e.tensor_tensor(
                    yT_sb[:, dv * TPC + qc * 256: dv * TPC + (qc + 1) * 256],
                    y_ps[:, dv * 256:(dv + 1) * 256], bc_sb[:], ALU.mult), "ev")
            ev_done[qc] = s.mark("ev")

        for m in range(8):
            s.dma(lambda e, m=m: e.dma_start(yTr[m], yT_sb[:, m * TPC:(m + 1) * TPC]), "kv",
                  after=[ev_done[NQC - 1]])
        s.emit(blk)
    return nc



def build_mix0():
    nc = bass.Bass("TRN2", target_bir_lowering=False)
    x = nc.dram_tensor("x", [TPC, D], F32, kind="ExternalInput").ap()
    yT = nc.dram_tensor("yT", [D, TPC], BF16, kind="ExternalInput").ap()
    xnT = nc.dram_tensor("xnT", [D, TPC], BF16, kind="ExternalInput").ap()
    owT = nc.dram_tensor("owT", [D, D], F32, kind="ExternalInput").ap()
    outb = nc.dram_tensor("outb", [1, D], F32, kind="ExternalInput").ap()
    fwT = nc.dram_tensor("fwT", [D, D], F32, kind="ExternalInput").ap()
    fscale = nc.dram_tensor("fscale", [1, D], F32, kind="ExternalInput").ap()
    fbias = nc.dram_tensor("fbias", [1, D], F32, kind="ExternalInput").ap()
    wqT = nc.dram_tensor("wqT", [D, D], F32, kind="ExternalInput").ap()
    wkT = nc.dram_tensor("wkT", [D, D], F32, kind="ExternalInput").ap()
    wvT = nc.dram_tensor("wvT", [D, D], F32, kind="ExternalInput").ap()
    wgT = nc.dram_tensor("wgT", [D, 16], F32, kind="ExternalInput").ap()
    bg = nc.dram_tensor("bg", [16, 1], F32, kind="ExternalInput").ap()
    out1 = nc.dram_tensor("out1", [TPC, D], F32, kind="ExternalOutput").ap()
    xn2tm = nc.dram_tensor("xn2tm", [TPC, D], BF16, kind="ExternalOutput").ap()
    xn2T = nc.dram_tensor("xn2T", [D, TPC], BF16, kind="ExternalOutput").ap()
    qfT = nc.dram_tensor("qfT", [D, TPC], BF16, kind="ExternalOutput").ap()
    kfT = nc.dram_tensor("kfT", [D, TPC], BF16, kind="ExternalOutput").ap()
    v1 = nc.dram_tensor("v1", [TPC, D], BF16, kind="ExternalOutput").ap()
    gate = nc.dram_tensor("gate", [16, TPC], F32, kind="ExternalOutput").ap()

    with ExitStack() as ctx:
        def sb(n, shp, dt): return ctx.enter_context(nc.sbuf_tensor(n, shp, dt))
        def ps(n, shp, dt): return ctx.enter_context(nc.psum_tensor(n, shp, dt))
        s = Seq(nc, ctx, chans=("g", "w"))
        yT_sb = sb("yT_sb", [128, 8 * TPC], BF16)
        xnT_sb = sb("xnT_sb", [128, 8 * TPC], BF16)
        wa = sb("wa", [128, 8 * D], BF16)
        wf = sb("wf", [128, 8 * D], BF16)
        wtmp = sb("wtmp", [128, 8 * D], F32)
        rowtmp = sb("rowtmp", [1, D], F32)
        outb_bf = sb("outb_bf", [1, D], BF16)
        ones1_bf = sb("ones1_bf", [1, 128], BF16)
        scale_mat = sb("scale_mat", [128, D], F32)
        bias_mat = sb("bias_mat", [128, D], F32)
        xt = sb("xt", [128, D], F32)
        ffn_sb = sb("ffn_sb", [128, D], F32)
        o1_sb = sb("o1_sb", [128, D], F32)
        st6 = sb("st6", [128, 12], F32)
        st2 = sb("st2", [128, 2], F32)
        tmp1 = sb("tmp1", [128, 1], F32)
        eps_col = sb("eps_col", [128, 1], F32)
        xn2_sb = sb("xn2_sb", [128, D], BF16)
        xn2T_sb = sb("xn2T_sb", [128, 8 * TPC], BF16)
        ps_a = ps("ps_a", [128, 1024], F32)
        ps_f = ps("ps_f", [128, 1024], F32)
        ps_b = ps("ps_b", [128, 1024], F32)
        blk = ctx.enter_context(nc.Block())

        s.op("vector", lambda e: e.memset(eps_col[:], 1e-5), "g")
        yTr = yT.rearrange("(m p) n -> m p n", p=128)
        xnTr = xnT.rearrange("(m p) n -> m p n", p=128)
        for j in range(8):
            s.dma(lambda e, j=j: e.dma_start(yT_sb[:, j * TPC:(j + 1) * TPC], yTr[j]), "g")
            s.dma(lambda e, j=j: e.dma_start(xnT_sb[:, j * TPC:(j + 1) * TPC], xnTr[j]), "g")
        load_weight_bf(s, owT, wa, wtmp, 8, D, chan="g")
        src = fwT.rearrange("(a p) n -> a p n", p=128)
        for a in range(8):
            s.dma(lambda e, a=a: e.dma_start(wtmp[:, a * D:(a + 1) * D], src[a]), "g")
        s.op("vector", lambda e: e.tensor_scalar(wf[:], wtmp[:], 0.0, None, ALU.is_ge), "g")
        s.op("vector", lambda e: e.tensor_scalar(wf[:], wf[:], 2.0, -1.0, ALU.mult, ALU.add), "g", force_wait=True)
        s.op("vector", lambda e: e.memset(ones1_bf[:], 1.0), "g")
        s.dma(lambda e: e.dma_start(rowtmp[:], outb), "g")
        s.op("vector", lambda e: e.tensor_copy(outb_bf[:], rowtmp[:]), "g")
        s.dma(lambda e: e.dma_start(rowtmp[:], fscale), "g", eng="scalar")
        onesf = sb("onesf", [1, 128], F32)
        s.op("vector", lambda e: e.memset(onesf[:], 1.0), "g")
        for nch in range(2):
            s.op("tensor", lambda e, nch=nch: e.matmul(
                ps_b[:, nch * 512:(nch + 1) * 512], onesf[:], rowtmp[:, nch * 512:(nch + 1) * 512],
                start=True, stop=True), "g", force_wait=(nch == 0))
        s.op("vector", lambda e: e.tensor_copy(scale_mat[:], ps_b[:]), "g")
        s.dma(lambda e: e.dma_start(rowtmp[:], fbias), "g")
        for nch in range(2):
            s.op("tensor", lambda e, nch=nch: e.matmul(
                ps_b[:, nch * 512:(nch + 1) * 512], onesf[:], rowtmp[:, nch * 512:(nch + 1) * 512],
                start=True, stop=True), "g", force_wait=(nch == 0))
        s.op("vector", lambda e: e.tensor_copy(bias_mat[:], ps_b[:]), "g")
        pre = s.mark("g")

        xr = x.rearrange("(t p) n -> t p n", p=128)
        o1r = out1.rearrange("(t p) n -> t p n", p=128)
        xn2r = xn2tm.rearrange("(t p) n -> t p n", p=128)
        for tt in range(NTT):
            s.dma(lambda e, tt=tt: e.dma_start(xt[:], xr[tt]), "g")
            for nch in range(2):
                for k in range(8):
                    s.op("tensor", lambda e, k=k, nch=nch, tt=tt: e.matmul(
                        ps_a[:, nch * 512:(nch + 1) * 512],
                        yT_sb[:, k * TPC + tt * 128: k * TPC + (tt + 1) * 128],
                        wa[:, k * D + nch * 512: k * D + (nch + 1) * 512],
                        start=(k == 0), stop=False), "g", force_wait=(k == 0 and nch == 0))
                s.op("tensor", lambda e, nch=nch: e.matmul(
                    ps_a[:, nch * 512:(nch + 1) * 512], ones1_bf[:],
                    outb_bf[:, nch * 512:(nch + 1) * 512],
                    start=False, stop=True, skip_group_check=True), "g")
            for nch in range(2):
                for k in range(8):
                    s.op("tensor", lambda e, k=k, nch=nch, tt=tt: e.matmul(
                        ps_f[:, nch * 512:(nch + 1) * 512],
                        xnT_sb[:, k * TPC + tt * 128: k * TPC + (tt + 1) * 128],
                        wf[:, k * D + nch * 512: k * D + (nch + 1) * 512],
                        start=(k == 0), stop=(k == 7)), "g")
            s.op("vector", lambda e: e.tensor_tensor(ffn_sb[:], ps_f[:], scale_mat[:], ALU.mult), "g", force_wait=True)
            s.op("vector", lambda e: e.tensor_tensor(ffn_sb[:], ffn_sb[:], bias_mat[:], ALU.add), "g")
            s.op("vector", lambda e: e.tensor_scalar(ffn_sb[:], ffn_sb[:], 100.0, -100.0, ALU.min, ALU.max), "g")
            s.op("vector", lambda e: e.tensor_tensor(o1_sb[:], xt[:], ps_a[:], ALU.add), "g")
            s.op("vector", lambda e: e.tensor_tensor(o1_sb[:], o1_sb[:], ffn_sb[:], ALU.add), "g")
            s.dma(lambda e, tt=tt: e.dma_start(o1r[tt], o1_sb[:]), "g")
            ln_tile(s, o1_sb, xn2_sb, st6, st2, tmp1, eps_col, "g")
            s.dma(lambda e, tt=tt: e.dma_start(xn2r[tt], xn2_sb[:]), "g")
            for j in range(8):
                s.dma(lambda e, tt=tt, j=j: e.dma_start(
                    xn2T_sb[:, j * TPC + tt * 128: j * TPC + (tt + 1) * 128],
                    xn2_sb[:, j * 128:(j + 1) * 128], transpose=True), "g")
        xn2done = s.mark("g")
        xn2Tr = xn2T.rearrange("(m p) n -> m p n", p=128)
        for j in range(8):
            s.dma(lambda e, j=j: e.dma_start(xn2Tr[j], xn2T_sb[:, j * TPC:(j + 1) * TPC]), "g")

        t_f1 = sb("t_f1", [128, 512], F32)
        t_f2 = sb("t_f2", [128, 512], F32)
        t_f3 = sb("t_f3", [128, 512], F32)
        fm_bf = sb("fm_bf", [128, 512], BF16)

        def fproj(whbm, out_hbm, scale, featmap):
            load_weight_bf(s, whbm, wa, wtmp, 8, D, chan="g")
            outr = out_hbm.rearrange("(m p) n -> m p n", p=128)
            idx = 0
            for m in range(8):
                for nch in range(4):
                    p = [ps_a, ps_f][idx % 2]
                    idx += 1
                    for k in range(8):
                        s.op("tensor", lambda e, p=p, k=k, m=m, nch=nch: e.matmul(
                            p[:, 0:512],
                            wa[:, k * D + m * 128: k * D + (m + 1) * 128],
                            xn2T_sb[:, k * TPC + nch * 512: k * TPC + (nch + 1) * 512],
                            start=(k == 0), stop=(k == 7)), "g", force_wait=(k == 0))
                    if featmap:
                        if scale is not None:
                            s.op("scalar", lambda e, p=p: e.mul(t_f1[:], p[:, 0:512], scale), "g")
                        else:
                            s.op("scalar", lambda e, p=p: e.copy(t_f1[:], p[:, 0:512]), "g")
                        s.op("vector", lambda e: e.tensor_scalar(t_f2[:], t_f1[:], 0.0, None, ALU.min), "g")
                        s.op("scalar", lambda e: e.activation(t_f3[:], t_f2[:], AF.Exp), "g")
                        s.op("vector", lambda e: e.tensor_scalar(t_f2[:], t_f1[:], 0.0, None, ALU.max), "g")
                        s.op("vector", lambda e: e.tensor_tensor(fm_bf[:], t_f3[:], t_f2[:], ALU.add), "g")
                    else:
                        s.op("scalar", lambda e, p=p: e.copy(fm_bf[:], p[:, 0:512]), "g")
                    s.dma(lambda e, m=m, nch=nch, outr=outr: e.dma_start(
                        outr[m][:, nch * 512:(nch + 1) * 512], fm_bf[:]), "g")

        fproj(wqT, qfT, 0.125, True)
        fproj(wkT, kfT, None, True)
        load_weight_bf(s, wvT, wa, wtmp, 8, D, chan="g")
        v1r = v1.rearrange("(t p) n -> t p n", p=128)
        idx = 0
        for tt in range(NTT):
            for nch in range(2):
                p = [ps_a, ps_f][idx % 2]
                idx += 1
                for k in range(8):
                    s.op("tensor", lambda e, p=p, k=k, tt=tt, nch=nch: e.matmul(
                        p[:, 0:512],
                        xn2T_sb[:, k * TPC + tt * 128: k * TPC + (tt + 1) * 128],
                        wa[:, k * D + nch * 512: k * D + (nch + 1) * 512],
                        start=(k == 0), stop=(k == 7)), "g", force_wait=(k == 0))
                s.op("scalar", lambda e, p=p: e.copy(fm_bf[:], p[:, 0:512]), "g")
                s.dma(lambda e, tt=tt, nch=nch: e.dma_start(
                    v1r[tt][:, nch * 512:(nch + 1) * 512], fm_bf[:]), "g")
        wg_sb = sb("wg_sb", [128, 8 * 16], BF16)
        wg_tmp = sb("wg_tmp", [128, 8 * 16], F32)
        bg_sb = sb("bg_sb", [16, 1], F32)
        g_out = sb("g_out", [16, 512], F32)
        wgr = wgT.rearrange("(a p) n -> a p n", p=128)
        for a in range(8):
            s.dma(lambda e, a=a: e.dma_start(wg_tmp[:, a * 16:(a + 1) * 16], wgr[a]), "g")
        s.op("vector", lambda e: e.tensor_copy(wg_sb[:], wg_tmp[:]), "g")
        s.dma(lambda e: e.dma_start(bg_sb[:], bg), "g")
        for nch in range(4):
            for k in range(8):
                s.op("tensor", lambda e, k=k, nch=nch: e.matmul(
                    ps_a[0:16, 0:512],
                    wg_sb[:, k * 16:(k + 1) * 16],
                    xn2T_sb[:, k * TPC + nch * 512: k * TPC + (nch + 1) * 512],
                    start=(k == 0), stop=(k == 7)), "g", force_wait=(k == 0))
            s.op("scalar", lambda e: e.activation(g_out[:], ps_a[0:16, 0:512], AF.Sigmoid, bias=bg_sb[:]), "g")
            s.dma(lambda e, nch=nch: e.dma_start(gate[:, nch * 512:(nch + 1) * 512], g_out[:]), "g")
        s.emit(blk)
    return nc



def build_scan():
    NH = 4
    NC_ = 64
    nc = bass.Bass("TRN2", target_bir_lowering=False)
    qf = nc.dram_tensor("qf", [NH, 64, SQ], BF16, kind="ExternalInput").ap()
    kf = nc.dram_tensor("kf", [NH, 64, SQ], BF16, kind="ExternalInput").ap()
    vh = nc.dram_tensor("vh", [NH, SQ, 64], BF16, kind="ExternalInput").ap()
    gate = nc.dram_tensor("gate", [NH, SQ], F32, kind="ExternalInput").ap()
    tri = nc.dram_tensor("tri", [128, 128], F32, kind="ExternalInput").ap()
    sel = nc.dram_tensor("sel", [NH, 512], F32, kind="ExternalInput").ap()
    yh = nc.dram_tensor("yh", [NH, SQ, 64], BF16, kind="ExternalOutput").ap()

    with ExitStack() as ctx:
        def sb(n, shp, dt): return ctx.enter_context(nc.sbuf_tensor(n, shp, dt))
        def ps(n, shp, dt): return ctx.enter_context(nc.psum_tensor(n, shp, dt))
        chans = ["g"] + [f"h{i}" for i in range(4)]
        s = Seq(nc, ctx, chans=chans)
        qf2 = [sb(f"qf{i}", [128, SQ], BF16) for i in range(2)]
        kf2 = [sb(f"kf{i}", [128, SQ], BF16) for i in range(2)]
        v_sb = [sb(f"v{i}", [128, NC_ * 64], BF16) for i in range(NH)]
        L4 = sb("L4", [NH, SQ], F32)
        sel_sb = sb("sel_sb", [NH, 512], F32)
        tri_sb = sb("tri_sb", [128, 128], F32)
        one1 = sb("one1", [1, 1], F32)
        Lrel4 = sb("Lrel4", [NH, 128], F32)
        p4 = sb("p4", [NH, 128], F32)
        r24 = sb("r24", [NH, 128], F32)
        negL4 = sb("negL4", [NH, 128], F32)
        D_sb = [sb(f"D{i}", [128, 128], F32) for i in range(NH)]
        R_sb = [sb(f"R{i}", [128, 128], BF16) for i in range(NH)]
        A_sb = [sb(f"A{i}", [128, 128], BF16) for i in range(NH)]
        qt_sb = [sb(f"qt{i}", [64, 128], BF16) for i in range(NH)]
        kh_sb = [sb(f"kh{i}", [128, 64], BF16) for i in range(NH)]
        St_sb = [sb(f"St{i}", [64, 64], F32) for i in range(NH)]
        S_fp = [sb(f"Sfp{i}", [64, 64], F32) for i in range(NH)]
        S_bf = [sb(f"Sbf{i}", [64, 64], BF16) for i in range(NH)]
        y_out = [sb(f"yo{i}", [128, 64], BF16) for i in range(NH)]
        PA = [ps(f"PA{i}", [128, 512], F32) for i in range(NH)]
        PB = [ps(f"PB{i}", [128, 512], F32) for i in range(NH)]
        blk = ctx.enter_context(nc.Block())

        s.op("vector", lambda e: e.memset(one1[:], 1.0), "g")
        s.dma(lambda e: e.dma_start(tri_sb[:], tri), "g")
        s.dma(lambda e: e.dma_start(sel_sb[:], sel), "g")
        for i in range(NH):
            p, sub = i // 2, i % 2
            s.dma(lambda e, i=i, p=p, sub=sub: e.dma_start(
                qf2[p][sub * 64:(sub + 1) * 64, :], qf[i]), "g")
            s.dma(lambda e, i=i, p=p, sub=sub: e.dma_start(
                kf2[p][sub * 64:(sub + 1) * 64, :], kf[i]), "g")
            vsrc = vh[i].rearrange("(c p) n -> p c n", p=128)
            s.dma(lambda e, i=i, vsrc=vsrc: e.dma_start(
                v_sb[i][:].rearrange("p (c n) -> p c n", n=64), vsrc), "g")
            s.op("vector", lambda e, i=i: e.memset(S_fp[i][:], 0.0), "g")
            s.op("vector", lambda e, i=i: e.memset(S_bf[i][:], 0.0), "g")
        with nc.sbuf_tensor("g4", [NH, SQ], F32) as g4:
            s.dma(lambda e: e.dma_start(g4[:], gate), "g")
            s.op("scalar", lambda e: e.activation(g4[:], g4[:], AF.Ln), "g", force_wait=True)
            s.op("vector", lambda e: e.tensor_tensor_scan(
                L4[:], g4[:], g4[:], 0.0, ALU.add, ALU.bypass), "g", force_wait=True)
        kftm_sb = [sb(f"kftm{i}", [128, NC_ * 64], BF16) for i in range(NH)]
        for i in range(NH):
            for c in range(NC_):
                s.dma(lambda e, i=i, c=c: e.dma_start(
                    kftm_sb[i][:, c * 64:(c + 1) * 64],
                    kf[i][:, c * 128:(c + 1) * 128], transpose=True), "g")
        pre = s.mark("g")

        for c in range(NC_):
            c0 = c * 128
            if c == 0:
                s.op("vector", lambda e, c0=c0: e.tensor_scalar(
                    Lrel4[:], L4[:, c0:c0 + 128], 0.0, None, ALU.subtract), "g", after=[pre])
            else:
                s.op("vector", lambda e, c0=c0: e.tensor_scalar(
                    Lrel4[:], L4[:, c0:c0 + 128], L4[:, c0 - 1:c0], None, ALU.subtract), "g",
                    force_wait=True)
            s.op("scalar", lambda e: e.activation(p4[:], Lrel4[:], AF.Exp), "g", force_wait=True)
            s.op("scalar", lambda e, c0=c0: e.activation(
                r24[:], L4[:, c0:c0 + 128], AF.Exp, scale=-1.0,
                bias=L4[:, c0 + 127:c0 + 128]), "g")
            s.op("scalar", lambda e, c0=c0: e.mul(negL4[:], L4[:, c0:c0 + 128], -1.0), "g")
            rows_m = s.mark("g")
            for i in range(NH):
                ch = f"h{i}"
                p, sub = i // 2, i % 2
                seli = sel_sb[:, i * 128:(i + 1) * 128]
                s.op("tensor", lambda e, i=i, c0=c0, seli=seli: e.matmul(
                    PA[i][:, 0:128], seli, L4[:, c0:c0 + 128],
                    start=True, stop=False), ch, after=[rows_m])
                s.op("tensor", lambda e, i=i, seli=seli: e.matmul(
                    PA[i][:, 0:128], negL4[:], seli,
                    start=False, stop=True, skip_group_check=True), ch)
                s.op("vector", lambda e, i=i: e.tensor_tensor(
                    D_sb[i][:], PA[i][:, 0:128], tri_sb[:], ALU.add), ch)
                s.op("scalar", lambda e, i=i: e.activation(R_sb[i][:], D_sb[i][:], AF.Exp), ch)
                s.op("tensor", lambda e, i=i, c0=c0, p=p, sub=sub: e.matmul(
                    PA[i][:, 128:256], kf2[p][sub * 64:(sub + 1) * 64, c0:c0 + 128],
                    qf2[p][sub * 64:(sub + 1) * 64, c0:c0 + 128], start=True, stop=True), ch)
                s.op("vector", lambda e, i=i: e.tensor_tensor(
                    A_sb[i][:], PA[i][:, 128:256], R_sb[i][:], ALU.mult), ch)
                s.op("tensor", lambda e, i=i, seli=seli: e.matmul(
                    PA[i][0:64, 256:384], seli[:, 0:64], p4[:],
                    start=True, stop=True), ch)
                s.op("vector", lambda e, i=i, c0=c0, p=p, sub=sub: e.tensor_tensor(
                    qt_sb[i][:], qf2[p][sub * 64:(sub + 1) * 64, c0:c0 + 128],
                    PA[i][0:64, 256:384], ALU.mult), ch)
                s.op("tensor", lambda e, i=i, c=c: e.matmul(
                    PB[i][:, 0:64], A_sb[i][:], v_sb[i][:, c * 64:(c + 1) * 64],
                    start=True, stop=False), ch)
                s.op("tensor", lambda e, i=i: e.matmul(
                    PB[i][:, 0:64], qt_sb[i][:], S_bf[i][:],
                    start=False, stop=True, skip_group_check=True), ch)
                s.op("scalar", lambda e, i=i: e.copy(y_out[i][:], PB[i][:, 0:64]), ch)
                s.dma(lambda e, i=i, c0=c0: e.dma_start(yh[i][c0:c0 + 128, :], y_out[i][:]), ch)
                s.op("tensor", lambda e, i=i, seli=seli: e.matmul(
                    PB[i][:, 128:129], r24[:], seli[:, 0:1], start=True, stop=True), ch)
                s.op("vector", lambda e, i=i, c=c: e.tensor_scalar(
                    kh_sb[i][:], kftm_sb[i][:, c * 64:(c + 1) * 64], PB[i][:, 128:129], None,
                    ALU.mult), ch)
                s.op("tensor", lambda e, i=i, c=c: e.matmul(
                    PB[i][0:64, 64:128], kh_sb[i][:], v_sb[i][:, c * 64:(c + 1) * 64],
                    start=True, stop=True), ch)
                s.op("tensor", lambda e, i=i, seli=seli, c0=c0: e.matmul(
                    PB[i][0:64, 129:130], seli[:, 0:64], p4[:, 127:128], start=True, stop=True), ch)
                s.op("vector", lambda e, i=i: e.tensor_scalar(
                    St_sb[i][:], S_fp[i][:], PB[i][0:64, 129:130], None, ALU.mult), ch)
                s.op("vector", lambda e, i=i: e.tensor_tensor(
                    S_fp[i][:], St_sb[i][:], PB[i][0:64, 64:128], ALU.add), ch)
                s.op("scalar", lambda e, i=i: e.copy(S_bf[i][:], S_fp[i][:]), ch)
        s.emit(blk)
    return nc



def build_mix1():
    NEX = 128
    NG = 64
    nc = bass.Bass("TRN2", target_bir_lowering=False)
    out1 = nc.dram_tensor("out1", [TPC, D], F32, kind="ExternalInput").ap()
    y = nc.dram_tensor("y", [TPC, D], BF16, kind="ExternalInput").ap()
    woT = nc.dram_tensor("woT", [D, D], F32, kind="ExternalInput").ap()
    xn2tm = nc.dram_tensor("xn2tm", [TPC, D], BF16, kind="ExternalInput").ap()
    xn2T = nc.dram_tensor("xn2T", [D, TPC], BF16, kind="ExternalInput").ap()
    wc = nc.dram_tensor("wc", [D, NEX], F32, kind="ExternalInput").ap()
    bc = nc.dram_tensor("bc", [NEX, 1], F32, kind="ExternalInput").ap()
    tie = nc.dram_tensor("tie", [128, 512], F32, kind="ExternalInput").ap()
    w1 = nc.dram_tensor("w1", [NEX, D, 32], F32, kind="ExternalInput").ap()
    w2t = nc.dram_tensor("w2t", [NEX, 32, D], F32, kind="ExternalInput").ap()
    b1 = nc.dram_tensor("b1", [NEX, 32], F32, kind="ExternalInput").ap()
    b2 = nc.dram_tensor("b2", [NEX, D], F32, kind="ExternalInput").ap()
    ident = nc.dram_tensor("ident", [128, 128], BF16, kind="ExternalInput").ap()
    out = nc.dram_tensor("out", [TPC, D], F32, kind="ExternalOutput").ap()
    o1b_hbm = nc.dram_tensor("o1b_scr", [TPC, D], F32, kind="Internal").ap()
    rp_hbm = nc.dram_tensor("rp_scr", [NEX, NG, D], BF16, kind="Internal").ap()

    with ExitStack() as ctx:
        def sb(n, shp, dt): return ctx.enter_context(nc.sbuf_tensor(n, shp, dt))
        def ps(n, shp, dt): return ctx.enter_context(nc.psum_tensor(n, shp, dt))
        s = Seq(nc, ctx, chans=("g", "e0", "e1"))
        ident_sb = sb("ident_sb", [128, 128], BF16)
        blk = ctx.enter_context(nc.Block())

        o1r = out1.rearrange("(t p) n -> t p n", p=128)
        o1br = o1b_hbm.rearrange("(t p) n -> t p n", p=128)
        with nc.sbuf_tensor("yT_sb", [128, 8 * TPC], BF16) as yT_sb, \
             nc.sbuf_tensor("wa", [128, 8 * D], BF16) as wa, \
             nc.sbuf_tensor("wtmp", [128, 8 * D], F32) as wtmp, \
             nc.sbuf_tensor("xt", [128, D], F32) as xt, \
             nc.sbuf_tensor("o1b_sb", [128, D], F32) as o1b_sb, \
             nc.psum_tensor("ps_a", [128, 1024], F32) as ps_a:
            for j in range(8):
                s.dma(lambda e, j=j: e.dma_start(
                    yT_sb[:, j * TPC:(j + 1) * TPC], y[:, j * 128:(j + 1) * 128], transpose=True), "g")
            load_weight_bf(s, woT, wa, wtmp, 8, D, chan="g")
            s.dma(lambda e: e.dma_start(ident_sb[:], ident), "g")
            for tt in range(NTT):
                s.dma(lambda e, tt=tt: e.dma_start(xt[:], o1r[tt]), "g")
                for nch in range(2):
                    for k in range(8):
                        s.op("tensor", lambda e, k=k, nch=nch, tt=tt: e.matmul(
                            ps_a[:, nch * 512:(nch + 1) * 512],
                            yT_sb[:, k * TPC + tt * 128: k * TPC + (tt + 1) * 128],
                            wa[:, k * D + nch * 512: k * D + (nch + 1) * 512],
                            start=(k == 0), stop=(k == 7)), "g", force_wait=(k == 0 and nch == 0))
                s.op("vector", lambda e: e.tensor_tensor(o1b_sb[:], xt[:], ps_a[:], ALU.add), "g",
                     force_wait=True)
                s.dma(lambda e, tt=tt: e.dma_start(o1br[tt], o1b_sb[:]), "g")
        stageA = s.mark("g")

        xn2tm_sb = sb("xn2tm_sb", [128, NTT * D], BF16)
        wc_sb = sb("wc_sb", [128, 8 * NEX], BF16)
        wc_tmp = sb("wc_tmp", [128, 8 * NEX], F32)
        bc_sb = sb("bc_sb", [128, 1], F32)
        tie_sb = sb("tie_sb", [128, 512], F32)
        lg_sb = sb("lg_sb", [128, TPC], F32)
        mx_sb = sb("mx_sb", [128, 1], F32)
        perm_sb = sb("perm_sb", [128, TPC], BF16)
        permT_blk = sb("permT_blk", [128, NTT * 512], BF16)
        perm_x = sb("perm_x", [128, 8 * 64 * 64], BF16)
        xr2 = xn2tm.rearrange("(t p) n -> t p n", p=128)
        xn2Tr = xn2T.rearrange("(m p) n -> m p n", p=128)
        for tt in range(NTT):
            s.dma(lambda e, tt=tt: e.dma_start(xn2tm_sb[:, tt * D:(tt + 1) * D], xr2[tt]), "g")
        wcr = wc.rearrange("(a p) n -> a p n", p=128)
        for a in range(8):
            s.dma(lambda e, a=a: e.dma_start(wc_tmp[:, a * NEX:(a + 1) * NEX], wcr[a]), "g")
        s.op("vector", lambda e: e.tensor_copy(wc_sb[:], wc_tmp[:]), "g")
        s.dma(lambda e: e.dma_start(bc_sb[:], bc), "g")
        s.dma(lambda e: e.dma_start(tie_sb[:], tie), "g")

        with nc.psum_tensor("ps_l", [128, 512], F32) as ps_l, \
             nc.psum_tensor("ps_pt", [32, 128], BF16) as ps_pt, \
             nc.sbuf_tensor("xn2T_sb", [128, 8 * TPC], BF16) as xn2T_sb:
            for j in range(8):
                s.dma(lambda e, j=j: e.dma_start(xn2T_sb[:, j * TPC:(j + 1) * TPC], xn2Tr[j]), "g")
            for nch in range(4):
                for k in range(8):
                    s.op("tensor", lambda e, k=k, nch=nch: e.matmul(
                        ps_l[:], wc_sb[:, k * NEX:(k + 1) * NEX],
                        xn2T_sb[:, k * TPC + nch * 512: k * TPC + (nch + 1) * 512],
                        start=(k == 0), stop=(k == 7)), "g", force_wait=(k == 0))
                s.op("vector", lambda e, nch=nch: e.tensor_scalar(
                    lg_sb[:, nch * 512:(nch + 1) * 512], ps_l[:], bc_sb[:], None, ALU.add), "g",
                    force_wait=True)
                s.op("vector", lambda e, nch=nch: e.tensor_tensor(
                    lg_sb[:, nch * 512:(nch + 1) * 512], lg_sb[:, nch * 512:(nch + 1) * 512],
                    tie_sb[:], ALU.add), "g")
            s.op("vector", lambda e: e.memset(permT_blk[:], 0.0), "g")
            for g in range(NG):
                s.op("vector", lambda e, g=g: e.reduce_max(
                    mx_sb[:], lg_sb[:, g * 32:(g + 1) * 32], axis=AX.X), "g")
                s.op("vector", lambda e, g=g: e.tensor_scalar(
                    perm_sb[:, g * 32:(g + 1) * 32], lg_sb[:, g * 32:(g + 1) * 32],
                    mx_sb[:], None, ALU.is_equal), "g", force_wait=True)
                s.op("tensor", lambda e, g=g: e.transpose(
                    ps_pt[:], perm_sb[:, g * 32:(g + 1) * 32], ident_sb[:]), "g", force_wait=True)
                gl, tt = g % 4, g // 4
                s.op("scalar", lambda e, gl=gl, tt=tt: e.copy(
                    permT_blk[gl * 32:(gl + 1) * 32, tt * 512 + gl * 128: tt * 512 + (gl + 1) * 128],
                    ps_pt[:]), "g", force_wait=True)
        permdone = s.mark("g")

        ptv = permT_blk[:].rearrange("p (t g e) -> p t g e", t=NTT, g=4, e=128)
        pxv = perm_x[:].rearrange("p (d g e) -> p d g e", d=8, g=64, e=64)
        ones64_bf = sb("ones64_bf", [1, 64], BF16)
        s.op("vector", lambda e: e.memset(ones64_bf[:], 1.0), "g")
        w1_bf = [sb(f"w1bf{i}", [128, 256], BF16) for i in range(2)]
        w2t_bf = [sb(f"w2tbf{i}", [32, D], BF16) for i in range(2)]
        b1_bf = [sb(f"b1bf{i}", [1, 32], BF16) for i in range(2)]
        b2_bf = [sb(f"b2bf{i}", [1, D], BF16) for i in range(2)]
        wtmp1 = [sb(f"wtmp1_{i}", [128, 256], F32) for i in range(2)]
        wtmp2 = [sb(f"wtmp2_{i}", [32, D], F32) for i in range(2)]
        btmp1 = [sb(f"btmp1_{i}", [1, 32], F32) for i in range(2)]
        btmp2 = [sb(f"btmp2_{i}", [1, D], F32) for i in range(2)]
        inner_bf = [sb(f"inner{i}", [32, 64], BF16) for i in range(2)]
        rp_bf = [sb(f"rp{i}", [64, D], BF16) for i in range(2)]

        with nc.psum_tensor("gps0", [128, 256], F32) as gps0, \
             nc.psum_tensor("gps1", [128, 256], F32) as gps1, \
             nc.psum_tensor("ip0", [32, 64], F32) as ip0, \
             nc.psum_tensor("ip1", [32, 64], F32) as ip1, \
             nc.psum_tensor("rpp0", [64, 1024], F32) as rpp0, \
             nc.psum_tensor("rpp1", [64, 1024], F32) as rpp1:
            gps = [gps0, gps1]
            ipp = [ip0, ip1]
            rpp = [rpp0, rpp1]
            for H in range(2):
                idx = 0
                for tt in range(NTT):
                    for dt in range(8):
                        p = gps[idx % 2]
                        ch = "g"
                        s.op("tensor", lambda e, p=p, tt=tt, dt=dt, H=H: e.matmul(
                            p[:], xn2tm_sb[:, tt * D + dt * 128: tt * D + (dt + 1) * 128],
                            ptv[:, tt, :, H * 64:(H + 1) * 64], start=True, stop=True),
                            ch, force_wait=True, after=[permdone] if idx == 0 else ())
                        s.op("scalar", lambda e, p=p, tt=tt, dt=dt: e.copy(
                            perm_x[:, dt * 4096 + tt * 256: dt * 4096 + (tt + 1) * 256], p[:]),
                            ch, force_wait=True)
                        idx += 1
                gath = s.mark("g")
                for el in range(64):
                    es = H * 64 + el
                    ch = f"e{el % 2}"
                    bi = el % 2
                    w1r = w1[es].rearrange("(a p) f -> a p f", p=128)
                    for a in range(8):
                        s.dma(lambda e, a=a, bi=bi, w1r=w1r: e.dma_start(
                            wtmp1[bi][:, a * 32:(a + 1) * 32], w1r[a]), ch,
                            after=[gath] if a == 0 else ())
                    s.op("vector", lambda e, bi=bi: e.tensor_copy(w1_bf[bi][:], wtmp1[bi][:]), ch)
                    s.dma(lambda e, es=es, bi=bi: e.dma_start(wtmp2[bi][:], w2t[es]), ch)
                    s.op("vector", lambda e, bi=bi: e.tensor_copy(w2t_bf[bi][:], wtmp2[bi][:]), ch)
                    s.dma(lambda e, es=es, bi=bi: e.dma_start(btmp1[bi][:], b1[es:es + 1, :]), ch)
                    s.op("vector", lambda e, bi=bi: e.tensor_copy(b1_bf[bi][:], btmp1[bi][:]), ch)
                    s.dma(lambda e, es=es, bi=bi: e.dma_start(btmp2[bi][:], b2[es:es + 1, :]), ch)
                    s.op("vector", lambda e, bi=bi: e.tensor_copy(b2_bf[bi][:], btmp2[bi][:]), ch)
                    for a in range(8):
                        s.op("tensor", lambda e, a=a, bi=bi, el=el: e.matmul(
                            ipp[bi][:], w1_bf[bi][:, a * 32:(a + 1) * 32],
                            pxv[:, a, :, el], start=(a == 0), stop=False), ch,
                            force_wait=(a == 0))
                    s.op("tensor", lambda e, bi=bi: e.matmul(
                        ipp[bi][:], b1_bf[bi][:], ones64_bf[:],
                        start=False, stop=True, skip_group_check=True), ch)
                    s.op("scalar", lambda e, bi=bi: e.activation(inner_bf[bi][:], ipp[bi][:], AF.Relu),
                         ch, force_wait=True)
                    for nch in range(2):
                        s.op("tensor", lambda e, bi=bi, nch=nch: e.matmul(
                            rpp[bi][:, nch * 512:(nch + 1) * 512], inner_bf[bi][:],
                            w2t_bf[bi][:, nch * 512:(nch + 1) * 512], start=True, stop=False),
                            ch, force_wait=(nch == 0))
                        s.op("tensor", lambda e, bi=bi, nch=nch: e.matmul(
                            rpp[bi][:, nch * 512:(nch + 1) * 512], ones64_bf[:],
                            b2_bf[bi][0:1, nch * 512:(nch + 1) * 512],
                            start=False, stop=True, skip_group_check=True), ch)
                    s.op("scalar", lambda e, bi=bi: e.copy(rp_bf[bi][:], rpp[bi][:]), ch,
                         force_wait=True)
                    s.dma(lambda e, es=es, bi=bi: e.dma_start(rp_hbm[es], rp_bf[bi][:]), ch)
        eAm, eBm = s.mark("e0"), s.mark("e1")

        rg = [sb(f"rg{i}", [128, D], BF16) for i in range(2)]
        o32 = sb("o32", [32, D], F32)
        f32_sb = sb("f32_sb", [32, D], F32)
        rpg = rp_hbm.rearrange("e g d -> g e d")
        outr = out.rearrange("(g p) n -> g p n", p=32)
        o1bg = o1b_hbm.rearrange("(g p) n -> g p n", p=32)
        with nc.psum_tensor("sc_ps", [32, 1024], F32) as sc_ps:
            for g in range(NG):
                bi = g % 2
                s.dma(lambda e, g=g, bi=bi: e.dma_start(rg[bi][:], rpg[g]), "g",
                      after=[eAm, eBm] if g == 0 else ())
                s.dma(lambda e, g=g: e.dma_start(o32[:], o1bg[g]), "g")
                for nch in range(2):
                    s.op("tensor", lambda e, g=g, bi=bi, nch=nch: e.matmul(
                        sc_ps[:, nch * 512:(nch + 1) * 512], perm_sb[:, g * 32:(g + 1) * 32],
                        rg[bi][:, nch * 512:(nch + 1) * 512], start=True, stop=True), "g",
                        force_wait=(nch == 0))
                s.op("vector", lambda e: e.tensor_tensor(f32_sb[:], o32[:], sc_ps[:], ALU.add), "g",
                     force_wait=True)
                s.dma(lambda e, g=g: e.dma_start(outr[g], f32_sb[:]), "g")
        s.emit(blk)
    return nc



_cache = {}


def _get(name, builder):
    if name not in _cache:
        _cache[name] = builder()
    return _cache[name]


def kernel_multi(L0_attn_x, L0_attn_weight_qkv, L0_attn_out_w, L0_attn_out_b, L0_ffn_weights,
           L0_ffn_scale, L0_ffn_bias, L1_attn_Wq, L1_attn_Wk, L1_attn_Wv, L1_attn_Wo,
           L1_attn_Wg, L1_attn_bg, L1_ffn_Wc, L1_ffn_bc, L1_ffn_W1, L1_ffn_b1,
           L1_ffn_W2, L1_ffn_b2):
    cores = list(range(NCORE))
    x = np.ascontiguousarray(np.asarray(L0_attn_x, dtype=np.float32))
    C = np.ascontiguousarray

    def slc(c):
        b, p = divmod(c, 4)
        return b, p * TPC

    nc1 = _get("qkv0", build_qkv0)
    wqkv = C(np.asarray(L0_attn_weight_qkv, dtype=np.float32))
    in1 = []
    for c in cores:
        b, r0 = slc(c)
        in1.append({"x": C(x[b, r0:r0 + TPC]), "wqkv": wqkv})
    r1 = run_bass_kernel_spmd(nc1, in1, core_ids=cores).results

    kT_full = [np.concatenate([r1[b * 4 + p]["kT"] for p in range(4)], axis=1) for b in range(2)]
    v_full = [np.concatenate([r1[b * 4 + p]["v"] for p in range(4)], axis=0) for b in range(2)]

    nc2 = _get("attn0", build_attn0)
    in2 = []
    for c in cores:
        b, r0 = slc(c)
        in2.append({"qT": r1[c]["qT"], "kT": C(kT_full[b]), "v": C(v_full[b])})
    r2 = run_bass_kernel_spmd(nc2, in2, core_ids=cores).results

    nc3 = _get("mix0", build_mix0)
    in3 = []
    owT = C(np.asarray(L0_attn_out_w, dtype=np.float32).T)
    outb = C(np.asarray(L0_attn_out_b, dtype=np.float32)[None, :])
    fwT = C(np.asarray(L0_ffn_weights, dtype=np.float32).T)
    fscale = C(np.asarray(L0_ffn_scale, dtype=np.float32)[None, :])
    fbias = C(np.asarray(L0_ffn_bias, dtype=np.float32)[None, :])
    wqT = C(np.asarray(L1_attn_Wq, dtype=np.float32).T)
    wkT = C(np.asarray(L1_attn_Wk, dtype=np.float32).T)
    wvT = C(np.asarray(L1_attn_Wv, dtype=np.float32).T)
    wgT = C(np.asarray(L1_attn_Wg, dtype=np.float32).T)
    bgc = C(np.asarray(L1_attn_bg, dtype=np.float32)[:, None])
    for c in cores:
        b, r0 = slc(c)
        in3.append({"x": C(x[b, r0:r0 + TPC]), "yT": r2[c]["yT"], "xnT": r1[c]["xnT"],
                    "owT": owT, "outb": outb, "fwT": fwT, "fscale": fscale, "fbias": fbias,
                    "wqT": wqT, "wkT": wkT, "wvT": wvT, "wgT": wgT, "bg": bgc})
    r3 = run_bass_kernel_spmd(nc3, in3, core_ids=cores).results

    qfT_full = [np.concatenate([r3[b * 4 + p]["qfT"] for p in range(4)], axis=1) for b in range(2)]
    kfT_full = [np.concatenate([r3[b * 4 + p]["kfT"] for p in range(4)], axis=1) for b in range(2)]
    v1_full = [np.concatenate([r3[b * 4 + p]["v1"] for p in range(4)], axis=0) for b in range(2)]
    gate_full = [np.concatenate([r3[b * 4 + p]["gate"] for p in range(4)], axis=1) for b in range(2)]

    nc4 = _get("scan", build_scan)
    tri = np.where(np.arange(128)[:, None] <= np.arange(128)[None, :], 0.0, -1e9).astype(np.float32)
    sel = np.zeros((4, 512), dtype=np.float32)
    for i in range(4):
        sel[i, i * 128:(i + 1) * 128] = 1.0
    in4 = []
    for c in cores:
        idxs = [4 * c + j for j in range(4)]
        qf = np.stack([qfT_full[i // 16][(i % 16) * 64:(i % 16 + 1) * 64] for i in idxs])
        kf = np.stack([kfT_full[i // 16][(i % 16) * 64:(i % 16 + 1) * 64] for i in idxs])
        vv = np.stack([v1_full[i // 16][:, (i % 16) * 64:(i % 16 + 1) * 64] for i in idxs])
        gg = np.stack([gate_full[i // 16][i % 16] for i in idxs])
        in4.append({"qf": C(qf), "kf": C(kf), "vh": C(vv), "gate": C(gg), "tri": tri, "sel": sel})
    r4 = run_bass_kernel_spmd(nc4, in4, core_ids=cores).results

    y_tm = [np.zeros((SQ, D), dtype=BF) for _ in range(2)]
    for c in cores:
        for j in range(4):
            i = 4 * c + j
            y_tm[i // 16][:, (i % 16) * 64:(i % 16 + 1) * 64] = r4[c]["yh"][j]

    nc5 = _get("mix1", build_mix1)
    woT = C(np.asarray(L1_attn_Wo, dtype=np.float32).T)
    wcf = C(np.asarray(L1_ffn_Wc, dtype=np.float32).reshape(D, 128))
    bcf = C(np.asarray(L1_ffn_bc, dtype=np.float32).reshape(128, 1))
    tie_pat = np.linspace(0.0, 1e-6, 32, dtype=np.float32)
    tie_mat = C(np.broadcast_to(np.tile(tie_pat, 16)[None, :], (128, 512)).copy())
    w1f = C(np.asarray(L1_ffn_W1, dtype=np.float32).reshape(128, D, 32))
    w2tf = C(np.asarray(L1_ffn_W2, dtype=np.float32).transpose(0, 1, 3, 2).reshape(128, 32, D))
    b1f = C(np.asarray(L1_ffn_b1, dtype=np.float32).reshape(128, 32))
    b2f = C(np.asarray(L1_ffn_b2, dtype=np.float32).reshape(128, D))
    identbf = np.eye(128, dtype=BF)
    in5 = []
    for c in cores:
        b, r0 = slc(c)
        in5.append({"out1": r3[c]["out1"], "y": C(y_tm[b][r0:r0 + TPC]), "woT": woT,
                    "xn2tm": r3[c]["xn2tm"], "xn2T": r3[c]["xn2T"], "wc": wcf, "bc": bcf,
                    "tie": tie_mat, "w1": w1f, "w2t": w2tf, "b1": b1f, "b2": b2f,
                    "ident": identbf})
    r5 = run_bass_kernel_spmd(nc5, in5, core_ids=cores).results

    outp = np.zeros((2, SQ, D), dtype=np.float32)
    for c in cores:
        b, r0 = slc(c)
        outp[b, r0:r0 + TPC] = r5[c]["out"]
    return outp



def build_fused():
    NEX, NG, NH, NC_ = 128, 64, 4, 64
    nc = bass.Bass("TRN2", target_bir_lowering=False)
    def inp(n, shp, dt=F32):
        return nc.dram_tensor(n, shp, dt, kind="ExternalInput").ap()
    x = inp("x", [TPC, D])
    wqkv = inp("wqkv", [D, 3 * D])
    owT = inp("owT", [D, D]); outb = inp("outb", [1, D])
    fwT = inp("fwT", [D, D]); fscale = inp("fscale", [1, D]); fbias = inp("fbias", [1, D])
    wqT = inp("wqT", [D, D]); wkT = inp("wkT", [D, D]); wvT = inp("wvT", [D, D])
    wgT = inp("wgT", [D, 16]); bg = inp("bg", [16, 1])
    woT = inp("woT", [D, D])
    wc = inp("wc", [D, NEX]); bc = inp("bc", [NEX, 1]); tie = inp("tie", [128, 512])
    w1 = inp("w1", [NEX, D, 32]); w2t = inp("w2t", [NEX, 32, D])
    b1 = inp("b1", [NEX, 32]); b2 = inp("b2", [NEX, D])
    tri = inp("tri", [128, 128]); sel = inp("sel", [NH, 512])
    ident = inp("ident", [128, 128], BF16)
    out = nc.dram_tensor("out", [TPC, D], F32, kind="ExternalOutput").ap()
    def intr(n, shp, dt=BF16):
        return nc.dram_tensor(n, shp, dt).ap()
    k_loc = intr("k_loc", [TPC, D]); v_loc = intr("v_loc", [TPC, D])
    k_full = intr("k_full", [SQ, D]); v_full = intr("v_full", [SQ, D])
    qf_loc = intr("qf_loc", [TPC, D]); kf_loc = intr("kf_loc", [TPC, D])
    v1_loc = intr("v1_loc", [TPC, D]); g_loc = intr("g_loc", [16, TPC], F32)
    qf_full = intr("qf_full", [SQ, D]); kf_full = intr("kf_full", [SQ, D])
    v1_full = intr("v1_full", [SQ, D]); g_all = intr("g_all", [64, TPC], F32)
    qf_mine = intr("qf_mine", [SQ, 256]); kf_mine = intr("kf_mine", [SQ, 256])
    v1_mine = intr("v1_mine", [SQ, 256]); g_mine = intr("g_mine", [NH, SQ], F32)
    y_loc = intr("y_loc", [SQ, 256]); y_all = intr("y_all", [4 * SQ, 256])
    y_mine = intr("y_mine", [TPC, D])
    out1 = intr("out1_i", [TPC, D], F32)
    xn2tm = intr("xn2tm_i", [TPC, D]); xn2T = intr("xn2T_i", [D, TPC])
    o1b_hbm = intr("o1b_scr", [TPC, D], F32)
    rp_hbm = intr("rp_scr", [NEX, NG, D])
    RGRP = [[0, 1, 2, 3], [4, 5, 6, 7]]

    with ExitStack() as ctx:
        chans = ["g", "cc", "pe", "ex", "ev", "kv", "e0", "e1"] + [f"h{i}" for i in range(4)]
        s = Seq(nc, ctx, chans=chans)
        blk = ctx.enter_context(nc.Block())
        cell = {}

        def AG(ins_, outs_, after):
            for a_, b_ in zip(ins_, outs_):
                s.op("gpsimd", lambda e, a_=a_, b_=b_: e.collective_compute(
                    "AllGather", ALU.bypass, replica_groups=RGRP,
                    ins=[a_.opt()], outs=[b_.opt()]),
                    "cc", inc=1, after=after)
            return s.mark("cc")

        def rdma(dst_fn, src_fn, after=()):
            for r in range(8):
                def fn(e, r=r):
                    if "pid" not in cell:
                        cell["pid"] = nc.partition_id([mybir.EngineType.SP])
                    try:
                        return e.dma_start(dst_fn(r % 4), src_fn(r % 4), cond=(cell["pid"] == r))
                    except Exception:
                        import sys
                        d0, s0 = dst_fn(r % 4), src_fn(r % 4)
                        print("RDMA FAIL r=", r, "dst=", d0.tensor.name, d0.shape,
                              "src=", s0.tensor.name, s0.shape, file=sys.stderr)
                        raise
                s.dma(fn, "g", after=after if r == 0 else ())

        xr = x.rearrange("(t p) n -> t p n", p=128)
        with nc.sbuf_tensor("qT_sb", [128, 8 * TPC], BF16) as qT_sb, \
             nc.sbuf_tensor("xnT_sb", [128, 8 * TPC], BF16) as xnT_sb:
            with ExitStack() as c1:
                def _sb1(n, shp, dt): return c1.enter_context(nc.sbuf_tensor(n, shp, dt))
                wsb = _sb1("wsb", [128, 8 * 3 * D], BF16)
                w_p1 = _sb1("wtmp1", [128, 4 * 3 * D], F32)
                x_p1 = _sb1("xt1", [128, D], F32)
                s6p1 = _sb1("st61", [128, 12], F32)
                s2p1 = _sb1("st21", [128, 2], F32)
                t1p1 = _sb1("tmp11", [128, 1], F32)
                eps_col = _sb1("eps_col1", [128, 1], F32)
                xn_tm = _sb1("xn_tm1", [128, D], BF16)
                out_bf = _sb1("out_bf1", [128, 2 * 512], BF16)
                pst = [c1.enter_context(nc.psum_tensor(f"ps1_{i}", [128, 512], F32)) for i in range(2)]
                s.op("vector", lambda e: e.memset(eps_col[:], 1e-5), "g")
                srcw = wqkv.rearrange("(a p) n -> a p n", p=128)
                for half in range(2):
                    for a in range(4):
                        s.dma(lambda e, a=a, half=half: e.dma_start(
                            w_p1[:, a * 3 * D:(a + 1) * 3 * D], srcw[half * 4 + a]), "g")
                    s.op("vector", lambda e, half=half: e.tensor_copy(
                        wsb[:, half * 4 * 3 * D:(half + 1) * 4 * 3 * D], w_p1[:]), "g",
                        force_wait=True)
                wdone = s.mark("g")
                for tt in range(NTT):
                    s.dma(lambda e, tt=tt: e.dma_start(x_p1[:], xr[tt]), "g")
                    ln_tile(s, x_p1, xn_tm, s6p1, s2p1, t1p1, eps_col, "g")
                    for j in range(8):
                        s.dma(lambda e, tt=tt, j=j: e.dma_start(
                            xnT_sb[:, j * TPC + tt * 128: j * TPC + (tt + 1) * 128],
                            xn_tm[:, j * 128:(j + 1) * 128], transpose=True), "g")
                xndone = s.mark("g")

                def proj1(out_hbm, col0, scale, n_m, mode, sbuf_out=None):
                    outr = out_hbm.rearrange("(m p) n -> m p n", p=128) if out_hbm is not None else None
                    idx = 0
                    for m in range(n_m):
                        nchunks = 4 if mode == "qk" else 2
                        for nch in range(nchunks):
                            p = pst[idx % 2]
                            idx += 1
                            for k in range(8):
                                if mode == "qk":
                                    lhs = wsb[:, k * 3 * D + col0 + m * 128: k * 3 * D + col0 + (m + 1) * 128]
                                    rhs = xnT_sb[:, k * TPC + nch * 512: k * TPC + (nch + 1) * 512]
                                else:
                                    lhs = xnT_sb[:, k * TPC + m * 128: k * TPC + (m + 1) * 128]
                                    rhs = wsb[:, k * 3 * D + col0 + nch * 512: k * 3 * D + col0 + (nch + 1) * 512]
                                s.op("tensor", lambda e, p=p, lhs=lhs, rhs=rhs, k=k: e.matmul(
                                    p[:], lhs, rhs, start=(k == 0), stop=(k == 7)), "g",
                                    after=[wdone, xndone] if k == 0 else ())
                            ob = out_bf[:, (idx % 2) * 512:(idx % 2) * 512 + 512]
                            if scale is None:
                                s.op("scalar", lambda e, p=p, ob=ob: e.copy(ob, p[:]), "g")
                            else:
                                s.op("scalar", lambda e, p=p, ob=ob, scale=scale: e.mul(ob, p[:], scale), "g")
                            if sbuf_out is not None:
                                s.op("vector", lambda e, m=m, nch=nch, ob=ob: e.tensor_copy(
                                    sbuf_out[:, m * TPC + nch * 512: m * TPC + (nch + 1) * 512], ob), "g")
                            else:
                                s.dma(lambda e, m=m, nch=nch, ob=ob, outr=outr: e.dma_start(
                                    outr[m][:, nch * 512:(nch + 1) * 512], ob), "g")

                proj1(None, 0, 1.0 / 32.0, 8, "qk", sbuf_out=qT_sb)
                proj1(k_loc, D, None, NTT, "v")
                proj1(v_loc, 2 * D, None, NTT, "v")
            kv_m = s.mark("g")
            ag1 = AG([k_loc, v_loc], [k_full, v_full], [kv_m])

            NQC = TPC // 256
            NKC = SQ // 1024
            NT = NKC * 8
            with nc.sbuf_tensor("yT_sb", [128, 8 * TPC], BF16) as yT_sb:
                with ExitStack() as c2:
                    def _sb2(n, shp, dt): return c2.enter_context(nc.sbuf_tensor(n, shp, dt))
                    def _ps2(n, shp, dt): return c2.enter_context(nc.psum_tensor(n, shp, dt))
                    kch = [_sb2(f"kch{i}", [128, 8 * 1024], BF16) for i in range(2)]
                    vch = [_sb2(f"vch{i}", [128, 8 * 1024], BF16) for i in range(2)]
                    E_sb = [_sb2(f"E{i}", [128, 256], BF16) for i in range(2)]
                    ones_col = _sb2("ones_col", [128, 1], BF16)
                    den_sb = _sb2("den_sb", [1, 256], F32)
                    bcf_sb = _sb2("bcf_sb", [128, 256], F32)
                    ones_row = _sb2("ones_row", [1, 128], F32)
                    y_ps = _ps2("y_ps", [128, 2048], F32)
                    s_ps = [_ps2(f"s_ps{i}", [128, 256], F32) for i in range(2)]
                    den_ps = _ps2("den_ps", [1, 256], F32)
                    bc_ps = _ps2("bc_ps", [128, 256], F32)
                    vr = v_full.rearrange("(c t p) n -> c t p n", p=128, t=8)
                    s.op("vector", lambda e: e.memset(ones_col[:], 1.0), "pe")
                    s.op("vector", lambda e: e.memset(ones_row[:], 1.0), "pe")
                    pre_done = s.mark("pe")
                    kv_loaded, pe_chunk_done, exp_done, y_done, ev_done, scores_done = {}, {}, {}, {}, {}, {}

                    def load_chunk(gi):
                        qc, c = divmod(gi, NKC)
                        buf = gi % 2
                        after = [pre_done, ag1]
                        if gi >= 2:
                            after.append(pe_chunk_done[gi - 2])
                        for j in range(8):
                            s.dma(lambda e, j=j, c=c, buf=buf: e.dma_start(
                                kch[buf][:, j * 1024:(j + 1) * 1024],
                                k_full[c * 1024:(c + 1) * 1024, j * 128:(j + 1) * 128],
                                transpose=True), "kv", after=after if j == 0 else ())
                        for t in range(8):
                            s.dma(lambda e, t=t, c=c, buf=buf: e.dma_start(
                                vch[buf][:, t * 1024:(t + 1) * 1024], vr[c, t]), "kv")
                        kv_loaded[gi] = s.mark("kv")

                    def scoresf(ti):
                        qc, t = divmod(ti, NT)
                        gi = qc * NKC + t // 8
                        buf = (t // 8) % 2
                        tl = t % 8
                        after = [kv_loaded[gi]]
                        if ti >= 2:
                            after.append(exp_done[ti - 2])
                        if t == 0 and qc > 0:
                            after.append(ev_done[qc - 1])
                        for k in range(8):
                            s.op("tensor", lambda e, k=k, buf=buf, tl=tl, qc=qc, t=t: e.matmul(
                                s_ps[t % 2][:],
                                kch[buf][:, k * 1024 + tl * 128: k * 1024 + (tl + 1) * 128],
                                qT_sb[:, k * TPC + qc * 256: k * TPC + qc * 256 + 256],
                                start=(k == 0), stop=(k == 7)), "pe", after=after if k == 0 else ())

                    def expf(ti):
                        qc, t = divmod(ti, NT)
                        after = [scores_done[ti]]
                        if ti >= 2:
                            after.append(y_done[ti - 2])
                        s.op("scalar", lambda e, t=t: e.activation(
                            E_sb[t % 2][:], s_ps[t % 2][:], AF.Exp), "ex", after=after)

                    def ymm(ti):
                        qc, t = divmod(ti, NT)
                        buf = (t // 8) % 2
                        tl = t % 8
                        after = [exp_done[ti]]
                        for dv in range(8):
                            s.op("tensor", lambda e, dv=dv, buf=buf, tl=tl, t=t: e.matmul(
                                y_ps[:, dv * 256:(dv + 1) * 256],
                                vch[buf][:, tl * 1024 + dv * 128: tl * 1024 + (dv + 1) * 128],
                                E_sb[t % 2][:],
                                start=(t == 0 and dv % 2 == 0), stop=(t == NT - 1),
                                skip_group_check=True), "pe", after=after if dv == 0 else ())
                        s.op("tensor", lambda e, t=t: e.matmul(
                            den_ps[:], ones_col[:], E_sb[t % 2][:],
                            start=(t == 0), stop=(t == NT - 1), skip_group_check=True), "pe")

                    for qc in range(NQC):
                        for ti in range(qc * NT, (qc + 1) * NT):
                            t = ti % NT
                            if t % 8 == 0:
                                load_chunk(qc * NKC + t // 8)
                            scoresf(ti)
                            scores_done[ti] = s.mark("pe")
                            expf(ti)
                            exp_done[ti] = s.mark("ex")
                            ymm(ti)
                            y_done[ti] = s.mark("pe")
                            if t % 8 == 7:
                                pe_chunk_done[qc * NKC + t // 8] = s.mark("pe")
                        s.op("vector", lambda e: e.reciprocal(den_sb[:], den_ps[:]), "ev",
                             after=[y_done[(qc + 1) * NT - 1]])
                        rec_m = s.mark("ev")
                        s.op("tensor", lambda e: e.matmul(bc_ps[:], ones_row[:], den_sb[:],
                                                          start=True, stop=True), "pe", after=[rec_m])
                        bc_m = s.mark("pe")
                        s.op("scalar", lambda e: e.copy(bcf_sb[:], bc_ps[:]), "ev", after=[bc_m])
                        for dv in range(8):
                            s.op("vector", lambda e, dv=dv, qc=qc: e.tensor_tensor(
                                yT_sb[:, dv * TPC + qc * 256: dv * TPC + (qc + 1) * 256],
                                y_ps[:, dv * 256:(dv + 1) * 256], bcf_sb[:], ALU.mult), "ev")
                        ev_done[qc] = s.mark("ev")
                flash_m = [s.mark("ev"), s.mark("pe"), s.mark("kv"), s.mark("ex")]

                with ExitStack() as c3:
                    def _sb3(n, shp, dt): return c3.enter_context(nc.sbuf_tensor(n, shp, dt))
                    def load_w_half(whbm, dst):
                        srcx = whbm.rearrange("(a p) n -> a p n", p=128)
                        for half in range(2):
                            for a in range(4):
                                s.dma(lambda e, a=a, half=half: e.dma_start(
                                    w_p3[:, a * D:(a + 1) * D], srcx[half * 4 + a]), "g",
                                    after=() if (half == 0 and a > 0) else ())
                            s.op("vector", lambda e, half=half: e.tensor_copy(
                                dst[:, half * 4 * D:(half + 1) * 4 * D], w_p3[:]), "g",
                                force_wait=True)
                    wa_p3 = _sb3("wa_p3", [128, 8 * D], BF16)
                    wf = _sb3("wf", [128, 8 * D], BF16)
                    w_p3 = _sb3("w_p3", [128, 4 * D], F32)
                    row_p3 = w_p3[0:1, 0:D]
                    outb_bf = _sb3("outb_bf", [1, D], BF16)
                    ones1_bf = _sb3("ones1_bf", [1, 128], BF16)
                    onesf = _sb3("onesf", [1, 128], F32)
                    scale_mat = _sb3("scale_mat", [128, D], F32)
                    bias_mat = _sb3("bias_mat", [128, D], F32)
                    x_p3 = _sb3("xt3", [128, D], F32)
                    ffn_sb = _sb3("ffn_sb", [128, D], F32)
                    o1_sb = _sb3("o1_sb", [128, D], F32)
                    s6p3 = _sb3("st63", [128, 12], F32)
                    s2p3 = _sb3("st23", [128, 2], F32)
                    t1p3 = _sb3("tmp13", [128, 1], F32)
                    eps2 = _sb3("eps2", [128, 1], F32)
                    xn2_sb = _sb3("xn2_sb", [128, D], BF16)
                    xn2T_p3 = _sb3("xn2T_p3", [128, 8 * TPC], BF16)
                    t_f1 = _sb3("t_f1", [128, 512], F32)
                    t_f2 = _sb3("t_f2", [128, 512], F32)
                    fm_bf = _sb3("fm_bf", [128, 512], BF16)
                    psa_p3 = c3.enter_context(nc.psum_tensor("ps_a3", [128, 1024], F32))
                    psf_p3 = c3.enter_context(nc.psum_tensor("ps_f3", [128, 1024], F32))
                    s.op("vector", lambda e: e.memset(eps2[:], 1e-5), "g", after=flash_m)
                    load_w_half(owT, wa_p3)
                    srcf = fwT.rearrange("(a p) n -> a p n", p=128)
                    for half in range(2):
                        for a in range(4):
                            s.dma(lambda e, a=a, half=half: e.dma_start(
                                w_p3[:, a * D:(a + 1) * D], srcf[half * 4 + a]), "g")
                        s.op("vector", lambda e, half=half: e.tensor_scalar(
                            wf[:, half * 4 * D:(half + 1) * 4 * D], w_p3[:], 0.0, None, ALU.is_ge),
                            "g", force_wait=True)
                    s.op("vector", lambda e: e.tensor_scalar(wf[:], wf[:], 2.0, -1.0, ALU.mult, ALU.add), "g", force_wait=True)
                    s.op("vector", lambda e: e.memset(ones1_bf[:], 1.0), "g")
                    s.op("vector", lambda e: e.memset(onesf[:], 1.0), "g")
                    s.dma(lambda e: e.dma_start(row_p3, outb), "g")
                    s.op("vector", lambda e: e.tensor_copy(outb_bf[:], row_p3), "g")
                    s.dma(lambda e: e.dma_start(row_p3, fscale), "g")
                    for nch in range(2):
                        s.op("tensor", lambda e, nch=nch: e.matmul(
                            psf_p3[:, nch * 512:(nch + 1) * 512], onesf[:], w_p3[0:1, nch * 512:(nch + 1) * 512],
                            start=True, stop=True), "g", force_wait=(nch == 0))
                    s.op("vector", lambda e: e.tensor_copy(scale_mat[:], psf_p3[:]), "g", force_wait=True)
                    s.dma(lambda e: e.dma_start(row_p3, fbias), "g")
                    for nch in range(2):
                        s.op("tensor", lambda e, nch=nch: e.matmul(
                            psf_p3[:, nch * 512:(nch + 1) * 512], onesf[:], w_p3[0:1, nch * 512:(nch + 1) * 512],
                            start=True, stop=True), "g", force_wait=(nch == 0))
                    s.op("vector", lambda e: e.tensor_copy(bias_mat[:], psf_p3[:]), "g", force_wait=True)

                    o1r_p3 = out1.rearrange("(t p) n -> t p n", p=128)
                    xn2r_p3 = xn2tm.rearrange("(t p) n -> t p n", p=128)
                    for tt in range(NTT):
                        s.dma(lambda e, tt=tt: e.dma_start(x_p3[:], xr[tt]), "g")
                        for nch in range(2):
                            for k in range(8):
                                s.op("tensor", lambda e, k=k, nch=nch, tt=tt: e.matmul(
                                    psa_p3[:, nch * 512:(nch + 1) * 512],
                                    yT_sb[:, k * TPC + tt * 128: k * TPC + (tt + 1) * 128],
                                    wa_p3[:, k * D + nch * 512: k * D + (nch + 1) * 512],
                                    start=(k == 0), stop=False), "g", force_wait=(k == 0 and nch == 0))
                            s.op("tensor", lambda e, nch=nch: e.matmul(
                                psa_p3[:, nch * 512:(nch + 1) * 512], ones1_bf[:],
                                outb_bf[0:1, nch * 512:(nch + 1) * 512],
                                start=False, stop=True, skip_group_check=True), "g")
                        for nch in range(2):
                            for k in range(8):
                                s.op("tensor", lambda e, k=k, nch=nch, tt=tt: e.matmul(
                                    psf_p3[:, nch * 512:(nch + 1) * 512],
                                    xnT_sb[:, k * TPC + tt * 128: k * TPC + (tt + 1) * 128],
                                    wf[:, k * D + nch * 512: k * D + (nch + 1) * 512],
                                    start=(k == 0), stop=(k == 7)), "g")
                        s.op("vector", lambda e: e.tensor_tensor(ffn_sb[:], psf_p3[:], scale_mat[:], ALU.mult), "g", force_wait=True)
                        s.op("vector", lambda e: e.tensor_tensor(ffn_sb[:], ffn_sb[:], bias_mat[:], ALU.add), "g")
                        s.op("vector", lambda e: e.tensor_scalar(ffn_sb[:], ffn_sb[:], 100.0, -100.0, ALU.min, ALU.max), "g")
                        s.op("vector", lambda e: e.tensor_tensor(o1_sb[:], x_p3[:], psa_p3[:], ALU.add), "g")
                        s.op("vector", lambda e: e.tensor_tensor(o1_sb[:], o1_sb[:], ffn_sb[:], ALU.add), "g")
                        s.dma(lambda e, tt=tt: e.dma_start(o1r_p3[tt], o1_sb[:]), "g")
                        ln_tile(s, o1_sb, xn2_sb, s6p3, s2p3, t1p3, eps2, "g")
                        s.dma(lambda e, tt=tt: e.dma_start(xn2r_p3[tt], xn2_sb[:]), "g")
                        for j in range(8):
                            s.dma(lambda e, tt=tt, j=j: e.dma_start(
                                xn2T_p3[:, j * TPC + tt * 128: j * TPC + (tt + 1) * 128],
                                xn2_sb[:, j * 128:(j + 1) * 128], transpose=True), "g")
                    xn2Tr = xn2T.rearrange("(m p) n -> m p n", p=128)
                    for j in range(8):
                        s.dma(lambda e, j=j: e.dma_start(xn2Tr[j], xn2T_p3[:, j * TPC:(j + 1) * TPC]), "g")

                    def fproj2(whbm, out_hbm, scale, featmap):
                        load_w_half(whbm, wa_p3)
                        outr = out_hbm.rearrange("(t p) n -> t p n", p=128)
                        idx = 0
                        for tt in range(NTT):
                            for nch in range(2):
                                p = [psa_p3, psf_p3][idx % 2]
                                idx += 1
                                for k in range(8):
                                    s.op("tensor", lambda e, p=p, k=k, tt=tt, nch=nch: e.matmul(
                                        p[:, 0:512],
                                        xn2T_p3[:, k * TPC + tt * 128: k * TPC + (tt + 1) * 128],
                                        wa_p3[:, k * D + nch * 512: k * D + (nch + 1) * 512],
                                        start=(k == 0), stop=(k == 7)), "g", force_wait=(k == 0))
                                if featmap:
                                    if scale is not None:
                                        s.op("scalar", lambda e, p=p: e.mul(t_f1[:], p[:, 0:512], scale), "g")
                                    else:
                                        s.op("scalar", lambda e, p=p: e.copy(t_f1[:], p[:, 0:512]), "g")
                                    s.op("vector", lambda e: e.tensor_scalar(t_f2[:], t_f1[:], 0.0, None, ALU.min), "g")
                                    s.op("scalar", lambda e: e.activation(t_f2[:], t_f2[:], AF.Exp), "g", force_wait=True)
                                    s.op("vector", lambda e: e.scalar_tensor_tensor(fm_bf[:], t_f1[:], 0.0, t_f2[:], ALU.max, ALU.add), "g", force_wait=True)
                                else:
                                    s.op("scalar", lambda e, p=p: e.copy(fm_bf[:], p[:, 0:512]), "g")
                                s.dma(lambda e, tt=tt, nch=nch, outr=outr: e.dma_start(
                                    outr[tt][:, nch * 512:(nch + 1) * 512], fm_bf[:]), "g")

                    fproj2(wqT, qf_loc, 0.125, True)
                    fproj2(wkT, kf_loc, None, True)
                    fproj2(wvT, v1_loc, None, False)
                    with ExitStack() as cg:
                        wg_sb = wa_p3[:, 0:128]
                        wg_tmp = w_p3[:, 0:128]
                        bg_sb = cg.enter_context(nc.sbuf_tensor("bg_sb", [16, 1], F32))
                        g_out = t_f1[0:16, 0:512]
                        wgr = wgT.rearrange("(a p) n -> a p n", p=128)
                        for a in range(8):
                            s.dma(lambda e, a=a: e.dma_start(wg_tmp[:, a * 16:(a + 1) * 16], wgr[a]), "g")
                        s.op("vector", lambda e: e.tensor_copy(wg_sb, wg_tmp), "g", force_wait=True)
                        s.dma(lambda e: e.dma_start(bg_sb[:], bg), "g")
                        for nch in range(4):
                            for k in range(8):
                                s.op("tensor", lambda e, k=k, nch=nch: e.matmul(
                                    psa_p3[0:16, 0:512],
                                    wg_sb[:, k * 16:(k + 1) * 16],
                                    xn2T_p3[:, k * TPC + nch * 512: k * TPC + (nch + 1) * 512],
                                    start=(k == 0), stop=(k == 7)), "g", force_wait=(k == 0))
                            s.op("scalar", lambda e: e.activation(g_out, psa_p3[0:16, 0:512], AF.Sigmoid, bias=bg_sb[:]), "g")
                            s.dma(lambda e, nch=nch: e.dma_start(g_loc[:, nch * 512:(nch + 1) * 512], g_out), "g")
        p3_m = s.mark("g")
        ag2 = AG([qf_loc, kf_loc], [qf_full, kf_full], [p3_m])
        ag2b = AG([v1_loc, g_loc], [v1_full, g_all], [p3_m])

        rdma(lambda rr: qf_mine[:, :], lambda rr: qf_full[:, rr * 256:(rr + 1) * 256], after=[ag2])
        rdma(lambda rr: kf_mine[:, :], lambda rr: kf_full[:, rr * 256:(rr + 1) * 256], after=[ag2])
        rdma(lambda rr: v1_mine[:, :], lambda rr: v1_full[:, rr * 256:(rr + 1) * 256], after=[ag2b])
        g_all_v = g_all.rearrange("(rp h) n -> rp h n", h=16)
        g_mine_v = g_mine.rearrange("i (rp n) -> rp i n", n=TPC)
        rdma(lambda rr: g_mine_v,
             lambda rr: g_all_v[:, rr * 4:(rr + 1) * 4, :],
             after=[ag2b])
        rs_m = s.mark("g")

        with ExitStack() as ctx4:
            def _sb4(n, shp, dt): return ctx4.enter_context(nc.sbuf_tensor(n, shp, dt))
            qf2 = [_sb4(f"qf2_{i}", [128, SQ], BF16) for i in range(2)]
            kf2 = [_sb4(f"kf2_{i}", [128, SQ], BF16) for i in range(2)]
            L4 = _sb4("L4", [NH, SQ], F32)
            sel_sb = _sb4("sel_sb", [NH, 512], F32)
            tri_sb = _sb4("tri_sb", [128, 128], F32)
            one1 = _sb4("one1", [1, 1], F32)
            Lrel4 = _sb4("Lrel4", [NH, 128], F32)
            p4t = _sb4("p4t", [NH, 128], F32)
            r24 = _sb4("r24", [NH, 128], F32)
            negL4 = _sb4("negL4", [NH, 128], F32)
            v_sb = [ctx4.enter_context(nc.sbuf_tensor(f"v4_{i}", [128, NC_ * 64], BF16)) for i in range(NH)]
            D_sb = [ctx4.enter_context(nc.sbuf_tensor(f"D4_{i}", [128, 128], F32)) for i in range(NH)]
            R_sb = [ctx4.enter_context(nc.sbuf_tensor(f"R4_{i}", [128, 128], BF16)) for i in range(NH)]
            A_sb = [ctx4.enter_context(nc.sbuf_tensor(f"A4_{i}", [128, 128], BF16)) for i in range(NH)]
            qt_sb = [ctx4.enter_context(nc.sbuf_tensor(f"qt4_{i}", [64, 128], BF16)) for i in range(NH)]
            kh_sb = [ctx4.enter_context(nc.sbuf_tensor(f"kh4_{i}", [128, 64], BF16)) for i in range(NH)]
            St_sb = [ctx4.enter_context(nc.sbuf_tensor(f"St4_{i}", [64, 64], F32)) for i in range(NH)]
            S_fp = [ctx4.enter_context(nc.sbuf_tensor(f"Sfp4_{i}", [64, 64], F32)) for i in range(NH)]
            S_bf = [ctx4.enter_context(nc.sbuf_tensor(f"Sbf4_{i}", [64, 64], BF16)) for i in range(NH)]
            y_out = [ctx4.enter_context(nc.sbuf_tensor(f"yo4_{i}", [128, 64], BF16)) for i in range(NH)]
            PA = [ctx4.enter_context(nc.psum_tensor(f"PA{i}", [128, 512], F32)) for i in range(NH)]
            PB = [ctx4.enter_context(nc.psum_tensor(f"PB{i}", [128, 512], F32)) for i in range(NH)]
            s.op("vector", lambda e: e.memset(one1[:], 1.0), "g")
            s.dma(lambda e: e.dma_start(tri_sb[:], tri), "g")
            s.dma(lambda e: e.dma_start(sel_sb[:], sel), "g")
            for pp in range(2):
                for c in range(NC_):
                    s.dma(lambda e, pp=pp, c=c: e.dma_start(
                        qf2[pp][:, c * 128:(c + 1) * 128],
                        qf_mine[c * 128:(c + 1) * 128, pp * 128:(pp + 1) * 128],
                        transpose=True), "g", after=[rs_m] if (pp == 0 and c == 0) else ())
                    s.dma(lambda e, pp=pp, c=c: e.dma_start(
                        kf2[pp][:, c * 128:(c + 1) * 128],
                        kf_mine[c * 128:(c + 1) * 128, pp * 128:(pp + 1) * 128],
                        transpose=True), "g")
            for i in range(NH):
                vsrc = v1_mine.rearrange("(c p) n -> p c n", p=128)[:, :, i * 64:(i + 1) * 64]
                s.dma(lambda e, i=i, vsrc=vsrc: e.dma_start(
                    v_sb[i][:].rearrange("p (c n) -> p c n", n=64), vsrc), "g")
                s.op("vector", lambda e, i=i: e.memset(S_fp[i][:], 0.0), "g")
                s.op("vector", lambda e, i=i: e.memset(S_bf[i][:], 0.0), "g")
            with nc.sbuf_tensor("g4", [NH, SQ], F32) as g4:
                s.dma(lambda e: e.dma_start(g4[:], g_mine), "g")
                s.op("scalar", lambda e: e.activation(g4[:], g4[:], AF.Ln), "g", force_wait=True)
                s.op("vector", lambda e: e.tensor_tensor_scan(
                    L4[:], g4[:], g4[:], 0.0, ALU.add, ALU.bypass), "g", force_wait=True)
            kftm_sb = [ctx4.enter_context(nc.sbuf_tensor(f"kftm{i}", [128, NC_ * 64], BF16)) for i in range(NH)]
            for i in range(NH):
                ksrc = kf_mine.rearrange("(c p) n -> p c n", p=128)[:, :, i * 64:(i + 1) * 64]
                s.dma(lambda e, i=i, ksrc=ksrc: e.dma_start(
                    kftm_sb[i][:].rearrange("p (c n) -> p c n", n=64), ksrc), "g")
            pre4 = s.mark("g")
            ylr = y_loc

            for c in range(NC_):
                c0 = c * 128
                if c == 0:
                    s.op("vector", lambda e, c0=c0: e.tensor_scalar(
                        Lrel4[:], L4[:, c0:c0 + 128], 0.0, None, ALU.subtract), "g", after=[pre4])
                else:
                    s.op("vector", lambda e, c0=c0: e.tensor_scalar(
                        Lrel4[:], L4[:, c0:c0 + 128], L4[:, c0 - 1:c0], None, ALU.subtract), "g",
                        force_wait=True)
                s.op("scalar", lambda e: e.activation(p4t[:], Lrel4[:], AF.Exp), "g", force_wait=True)
                s.op("scalar", lambda e, c0=c0: e.activation(
                    r24[:], L4[:, c0:c0 + 128], AF.Exp, scale=-1.0,
                    bias=L4[:, c0 + 127:c0 + 128]), "g")
                s.op("scalar", lambda e, c0=c0: e.mul(negL4[:], L4[:, c0:c0 + 128], -1.0), "g")
                rows_m = s.mark("g")
                for i in range(NH):
                    ch = f"h{i}"
                    pp, sub = i // 2, i % 2
                    seli = sel_sb[:, i * 128:(i + 1) * 128]
                    s.op("tensor", lambda e, i=i, c0=c0, seli=seli: e.matmul(
                        PA[i][:, 0:128], seli, L4[:, c0:c0 + 128],
                        start=True, stop=False), ch, after=[rows_m])
                    s.op("tensor", lambda e, i=i, seli=seli: e.matmul(
                        PA[i][:, 0:128], negL4[:], seli,
                        start=False, stop=True, skip_group_check=True), ch)
                    s.op("vector", lambda e, i=i: e.tensor_tensor(
                        D_sb[i][:], PA[i][:, 0:128], tri_sb[:], ALU.add), ch)
                    s.op("scalar", lambda e, i=i: e.activation(R_sb[i][:], D_sb[i][:], AF.Exp), ch)
                    s.op("tensor", lambda e, i=i, c0=c0, pp=pp, sub=sub: e.matmul(
                        PA[i][:, 128:256], kf2[pp][sub * 64:(sub + 1) * 64, c0:c0 + 128],
                        qf2[pp][sub * 64:(sub + 1) * 64, c0:c0 + 128], start=True, stop=True), ch)
                    s.op("vector", lambda e, i=i: e.tensor_tensor(
                        A_sb[i][:], PA[i][:, 128:256], R_sb[i][:], ALU.mult), ch)
                    s.op("tensor", lambda e, i=i, seli=seli: e.matmul(
                        PA[i][0:64, 256:384], seli[:, 0:64], p4t[:],
                        start=True, stop=True), ch)
                    s.op("vector", lambda e, i=i, c0=c0, pp=pp, sub=sub: e.tensor_tensor(
                        qt_sb[i][:], qf2[pp][sub * 64:(sub + 1) * 64, c0:c0 + 128],
                        PA[i][0:64, 256:384], ALU.mult), ch)
                    s.op("tensor", lambda e, i=i, c=c: e.matmul(
                        PB[i][:, 0:64], A_sb[i][:], v_sb[i][:, c * 64:(c + 1) * 64],
                        start=True, stop=False), ch)
                    s.op("tensor", lambda e, i=i: e.matmul(
                        PB[i][:, 0:64], qt_sb[i][:], S_bf[i][:],
                        start=False, stop=True, skip_group_check=True), ch)
                    s.op("scalar", lambda e, i=i: e.copy(y_out[i][:], PB[i][:, 0:64]), ch)
                    s.dma(lambda e, i=i, c0=c0: e.dma_start(
                        ylr[c0:c0 + 128, i * 64:(i + 1) * 64], y_out[i][:]), ch)
                    s.op("tensor", lambda e, i=i, seli=seli: e.matmul(
                        PB[i][:, 128:129], r24[:], seli[:, 0:1], start=True, stop=True), ch)
                    s.op("vector", lambda e, i=i, c=c: e.tensor_scalar(
                        kh_sb[i][:], kftm_sb[i][:, c * 64:(c + 1) * 64], PB[i][:, 128:129], None,
                        ALU.mult), ch)
                    s.op("tensor", lambda e, i=i, c=c: e.matmul(
                        PB[i][0:64, 64:128], kh_sb[i][:], v_sb[i][:, c * 64:(c + 1) * 64],
                        start=True, stop=True), ch)
                    s.op("tensor", lambda e, i=i, seli=seli: e.matmul(
                        PB[i][0:64, 129:130], seli[:, 0:64], p4t[:, 127:128], start=True, stop=True), ch)
                    s.op("vector", lambda e, i=i: e.tensor_scalar(
                        St_sb[i][:], S_fp[i][:], PB[i][0:64, 129:130], None, ALU.mult), ch)
                    s.op("vector", lambda e, i=i: e.tensor_tensor(
                        S_fp[i][:], St_sb[i][:], PB[i][0:64, 64:128], ALU.add), ch)
                    s.op("scalar", lambda e, i=i: e.copy(S_bf[i][:], S_fp[i][:]), ch)
        scan_m = [s.mark(f"h{i}") for i in range(NH)]
        ag3 = AG([y_loc], [y_all], scan_m + [s.mark("g")])
        y_all_v = y_all.rearrange("(rp t) n -> rp t n", t=SQ)
        y_mine_v = y_mine.rearrange("t (rp n) -> rp t n", n=256)
        rdma(lambda rr: y_mine_v,
             lambda rr: y_all_v[:, rr * TPC:(rr + 1) * TPC, :],
             after=[ag3])
        y_m = s.mark("g")

        with ExitStack() as c5:
            ident_sb = c5.enter_context(nc.sbuf_tensor("ident_sb", [128, 128], BF16))
            o1r_p5 = out1.rearrange("(t p) n -> t p n", p=128)
            o1br = o1b_hbm.rearrange("(t p) n -> t p n", p=128)
            with ExitStack() as c5a:
                yT_p5 = c5a.enter_context(nc.sbuf_tensor("yT_sb5", [128, 8 * TPC], BF16))
                wa_p5 = c5a.enter_context(nc.sbuf_tensor("wa5", [128, 8 * D], BF16))
                wt_p5 = c5a.enter_context(nc.sbuf_tensor("wtmp5", [128, 8 * D], F32))
                xt_p5 = c5a.enter_context(nc.sbuf_tensor("xt5", [128, D], F32))
                o1b_sb = c5a.enter_context(nc.sbuf_tensor("o1b_sb5", [128, D], F32))
                psa_p5 = c5a.enter_context(nc.psum_tensor("ps_a5", [128, 1024], F32))
                for j in range(8):
                    s.dma(lambda e, j=j: e.dma_start(
                        yT_p5[:, j * TPC:(j + 1) * TPC], y_mine[:, j * 128:(j + 1) * 128],
                        transpose=True), "g", after=[y_m] if j == 0 else ())
                load_weight_bf(s, woT, wa_p5, wt_p5, 8, D, chan="g")
                s.dma(lambda e: e.dma_start(ident_sb[:], ident), "g")
                for tt in range(NTT):
                    s.dma(lambda e, tt=tt: e.dma_start(xt_p5[:], o1r_p5[tt]), "g")
                    for nch in range(2):
                        for k in range(8):
                            s.op("tensor", lambda e, k=k, nch=nch, tt=tt: e.matmul(
                                psa_p5[:, nch * 512:(nch + 1) * 512],
                                yT_p5[:, k * TPC + tt * 128: k * TPC + (tt + 1) * 128],
                                wa_p5[:, k * D + nch * 512: k * D + (nch + 1) * 512],
                                start=(k == 0), stop=(k == 7)), "g", force_wait=(k == 0 and nch == 0))
                    s.op("vector", lambda e: e.tensor_tensor(o1b_sb[:], xt_p5[:], psa_p5[:], ALU.add), "g",
                         force_wait=True)
                    s.dma(lambda e, tt=tt: e.dma_start(o1br[tt], o1b_sb[:]), "g")
            stageA = s.mark("g")

            xn2tm_sb = c5.enter_context(nc.sbuf_tensor("xn2tm_sb5", [128, NTT * D], BF16))
            wc_sb = c5.enter_context(nc.sbuf_tensor("wc_sb5", [128, 8 * NEX], BF16))
            wc_tmp = c5.enter_context(nc.sbuf_tensor("wc_tmp5", [128, 8 * NEX], F32))
            bc_sb = c5.enter_context(nc.sbuf_tensor("bc_sb5", [128, 1], F32))
            tie_sb = c5.enter_context(nc.sbuf_tensor("tie_sb5", [128, 512], F32))
            lg_sb = c5.enter_context(nc.sbuf_tensor("lg_sb5", [128, TPC], F32))
            mx_sb = c5.enter_context(nc.sbuf_tensor("mx_sb5", [128, 1], F32))
            perm_sb = c5.enter_context(nc.sbuf_tensor("perm_sb5", [128, TPC], BF16))
            permT_blk = c5.enter_context(nc.sbuf_tensor("permT5", [128, NTT * 512], BF16))
            perm_x = c5.enter_context(nc.sbuf_tensor("perm_x5", [128, 8 * 64 * 64], BF16))
            xr2 = xn2tm.rearrange("(t p) n -> t p n", p=128)
            xn2Tr = xn2T.rearrange("(m p) n -> m p n", p=128)
            for tt in range(NTT):
                s.dma(lambda e, tt=tt: e.dma_start(xn2tm_sb[:, tt * D:(tt + 1) * D], xr2[tt]), "g")
            wcr = wc.rearrange("(a p) n -> a p n", p=128)
            for a in range(8):
                s.dma(lambda e, a=a: e.dma_start(wc_tmp[:, a * NEX:(a + 1) * NEX], wcr[a]), "g")
            s.op("vector", lambda e: e.tensor_copy(wc_sb[:], wc_tmp[:]), "g")
            s.dma(lambda e: e.dma_start(bc_sb[:], bc), "g")
            s.dma(lambda e: e.dma_start(tie_sb[:], tie), "g")
            with ExitStack() as c5l:
                ps_l = c5l.enter_context(nc.psum_tensor("ps_l", [128, 512], F32))
                ps_pt = c5l.enter_context(nc.psum_tensor("ps_pt", [32, 128], BF16))
                xn2T_p5 = c5l.enter_context(nc.sbuf_tensor("xn2T_sb5", [128, 8 * TPC], BF16))
                for j in range(8):
                    s.dma(lambda e, j=j: e.dma_start(xn2T_p5[:, j * TPC:(j + 1) * TPC], xn2Tr[j]), "g")
                for nch in range(4):
                    for k in range(8):
                        s.op("tensor", lambda e, k=k, nch=nch: e.matmul(
                            ps_l[:], wc_sb[:, k * NEX:(k + 1) * NEX],
                            xn2T_p5[:, k * TPC + nch * 512: k * TPC + (nch + 1) * 512],
                            start=(k == 0), stop=(k == 7)), "g", force_wait=(k == 0))
                    s.op("vector", lambda e, nch=nch: e.tensor_scalar(
                        lg_sb[:, nch * 512:(nch + 1) * 512], ps_l[:], bc_sb[:], None, ALU.add), "g",
                        force_wait=True)
                    s.op("vector", lambda e, nch=nch: e.tensor_tensor(
                        lg_sb[:, nch * 512:(nch + 1) * 512], lg_sb[:, nch * 512:(nch + 1) * 512],
                        tie_sb[:], ALU.add), "g")
                s.op("vector", lambda e: e.memset(permT_blk[:], 0.0), "g")
                for g in range(NG):
                    s.op("vector", lambda e, g=g: e.reduce_max(
                        mx_sb[:], lg_sb[:, g * 32:(g + 1) * 32], axis=AX.X), "g")
                    s.op("vector", lambda e, g=g: e.tensor_scalar(
                        perm_sb[:, g * 32:(g + 1) * 32], lg_sb[:, g * 32:(g + 1) * 32],
                        mx_sb[:], None, ALU.is_equal), "g", force_wait=True)
                    s.op("tensor", lambda e, g=g: e.transpose(
                        ps_pt[:], perm_sb[:, g * 32:(g + 1) * 32], ident_sb[:]), "g", force_wait=True)
                    gl, tt = g % 4, g // 4
                    s.op("scalar", lambda e, gl=gl, tt=tt: e.copy(
                        permT_blk[gl * 32:(gl + 1) * 32, tt * 512 + gl * 128: tt * 512 + (gl + 1) * 128],
                        ps_pt[:]), "g", force_wait=True)
            permdone = s.mark("g")

            ptv = permT_blk[:].rearrange("p (t g e) -> p t g e", t=NTT, g=4, e=128)
            pxv = perm_x[:].rearrange("p (d g e) -> p d g e", d=8, g=64, e=64)
            ones64_bf = c5.enter_context(nc.sbuf_tensor("ones64b5", [1, 64], BF16))
            s.op("vector", lambda e: e.memset(ones64_bf[:], 1.0), "g")
            w1_bf = [c5.enter_context(nc.sbuf_tensor(f"w1bf5{i}", [128, 256], BF16)) for i in range(2)]
            w2t_bf = [c5.enter_context(nc.sbuf_tensor(f"w2tbf5{i}", [32, D], BF16)) for i in range(2)]
            b1_bf = [c5.enter_context(nc.sbuf_tensor(f"b1bf5{i}", [1, 32], BF16)) for i in range(2)]
            b2_bf = [c5.enter_context(nc.sbuf_tensor(f"b2bf5{i}", [1, D], BF16)) for i in range(2)]
            wtmp1 = [c5.enter_context(nc.sbuf_tensor(f"wtmp15{i}", [128, 256], F32)) for i in range(2)]
            wtmp2 = [c5.enter_context(nc.sbuf_tensor(f"wtmp25{i}", [32, D], F32)) for i in range(2)]
            btmp1 = [c5.enter_context(nc.sbuf_tensor(f"btmp15{i}", [1, 32], F32)) for i in range(2)]
            btmp2 = [c5.enter_context(nc.sbuf_tensor(f"btmp25{i}", [1, D], F32)) for i in range(2)]
            inner_bf = [c5.enter_context(nc.sbuf_tensor(f"inner5{i}", [32, 64], BF16)) for i in range(2)]
            rp_bf = [c5.enter_context(nc.sbuf_tensor(f"rp5{i}", [64, D], BF16)) for i in range(2)]
            with ExitStack() as c5e:
                gps = [c5e.enter_context(nc.psum_tensor(f"gps{i}", [128, 256], F32)) for i in range(2)]
                ipp = [c5e.enter_context(nc.psum_tensor(f"ip{i}", [32, 64], F32)) for i in range(2)]
                rpp = [c5e.enter_context(nc.psum_tensor(f"rpp{i}", [64, 1024], F32)) for i in range(2)]
                for H in range(2):
                    idx = 0
                    for tt in range(NTT):
                        for dt in range(8):
                            p = gps[idx % 2]
                            s.op("tensor", lambda e, p=p, tt=tt, dt=dt, H=H: e.matmul(
                                p[:], xn2tm_sb[:, tt * D + dt * 128: tt * D + (dt + 1) * 128],
                                ptv[:, tt, :, H * 64:(H + 1) * 64], start=True, stop=True),
                                "g", force_wait=True, after=[permdone] if idx == 0 else ())
                            s.op("scalar", lambda e, p=p, tt=tt, dt=dt: e.copy(
                                perm_x[:, dt * 4096 + tt * 256: dt * 4096 + (tt + 1) * 256], p[:]),
                                "g", force_wait=True)
                            idx += 1
                    gath = s.mark("g")
                    for el in range(64):
                        es = H * 64 + el
                        ch = f"e{el % 2}"
                        bi = el % 2
                        w1r = w1[es].rearrange("(a p) f -> a p f", p=128)
                        for a in range(8):
                            s.dma(lambda e, a=a, bi=bi, w1r=w1r: e.dma_start(
                                wtmp1[bi][:, a * 32:(a + 1) * 32], w1r[a]), ch,
                                after=[gath] if a == 0 else ())
                        s.op("vector", lambda e, bi=bi: e.tensor_copy(w1_bf[bi][:], wtmp1[bi][:]), ch)
                        s.dma(lambda e, es=es, bi=bi: e.dma_start(wtmp2[bi][:], w2t[es]), ch)
                        s.op("vector", lambda e, bi=bi: e.tensor_copy(w2t_bf[bi][:], wtmp2[bi][:]), ch)
                        s.dma(lambda e, es=es, bi=bi: e.dma_start(btmp1[bi][:], b1[es:es + 1, :]), ch)
                        s.op("vector", lambda e, bi=bi: e.tensor_copy(b1_bf[bi][:], btmp1[bi][:]), ch)
                        s.dma(lambda e, es=es, bi=bi: e.dma_start(btmp2[bi][:], b2[es:es + 1, :]), ch)
                        s.op("vector", lambda e, bi=bi: e.tensor_copy(b2_bf[bi][:], btmp2[bi][:]), ch)
                        for a in range(8):
                            s.op("tensor", lambda e, a=a, bi=bi, el=el: e.matmul(
                                ipp[bi][:], w1_bf[bi][:, a * 32:(a + 1) * 32],
                                pxv[:, a, :, el], start=(a == 0), stop=False), ch,
                                force_wait=(a == 0))
                        s.op("tensor", lambda e, bi=bi: e.matmul(
                            ipp[bi][:], b1_bf[bi][:], ones64_bf[:],
                            start=False, stop=True, skip_group_check=True), ch)
                        s.op("scalar", lambda e, bi=bi: e.activation(inner_bf[bi][:], ipp[bi][:], AF.Relu),
                             ch, force_wait=True)
                        for nch in range(2):
                            s.op("tensor", lambda e, bi=bi, nch=nch: e.matmul(
                                rpp[bi][:, nch * 512:(nch + 1) * 512], inner_bf[bi][:],
                                w2t_bf[bi][0:32, nch * 512:(nch + 1) * 512], start=True, stop=False),
                                ch, force_wait=(nch == 0))
                            s.op("tensor", lambda e, bi=bi, nch=nch: e.matmul(
                                rpp[bi][:, nch * 512:(nch + 1) * 512], ones64_bf[:],
                                b2_bf[bi][0:1, nch * 512:(nch + 1) * 512],
                                start=False, stop=True, skip_group_check=True), ch)
                        s.op("scalar", lambda e, bi=bi: e.copy(rp_bf[bi][:], rpp[bi][:]), ch,
                             force_wait=True)
                        s.dma(lambda e, es=es, bi=bi: e.dma_start(rp_hbm[es], rp_bf[bi][:]), ch)
            eAm, eBm = s.mark("e0"), s.mark("e1")

            with ExitStack() as c5s:
                rg = [c5s.enter_context(nc.sbuf_tensor(f"rg{i}", [128, D], BF16)) for i in range(2)]
                o32 = c5s.enter_context(nc.sbuf_tensor("o32", [32, D], F32))
                f32_sb = c5s.enter_context(nc.sbuf_tensor("f32_sb", [32, D], F32))
                sc_ps = c5s.enter_context(nc.psum_tensor("sc_ps", [32, 1024], F32))
                rpg = rp_hbm.rearrange("e g d -> g e d")
                outr = out.rearrange("(g p) n -> g p n", p=32)
                o1bg = o1b_hbm.rearrange("(g p) n -> g p n", p=32)
                for g in range(NG):
                    bi = g % 2
                    s.dma(lambda e, g=g, bi=bi: e.dma_start(rg[bi][:], rpg[g]), "g",
                          after=[eAm, eBm, stageA] if g == 0 else ())
                    s.dma(lambda e, g=g: e.dma_start(o32[:], o1bg[g]), "g")
                    for nch in range(2):
                        s.op("tensor", lambda e, g=g, bi=bi, nch=nch: e.matmul(
                            sc_ps[:, nch * 512:(nch + 1) * 512], perm_sb[:, g * 32:(g + 1) * 32],
                            rg[bi][:, nch * 512:(nch + 1) * 512], start=True, stop=True), "g",
                            force_wait=(nch == 0))
                    s.op("vector", lambda e: e.tensor_tensor(f32_sb[:], o32[:], sc_ps[:], ALU.add), "g",
                         force_wait=True)
                    s.dma(lambda e, g=g: e.dma_start(outr[g], f32_sb[:]), "g")
        s.emit(blk)
    return nc


def kernel(L0_attn_x, L0_attn_weight_qkv, L0_attn_out_w, L0_attn_out_b, L0_ffn_weights,
           L0_ffn_scale, L0_ffn_bias, L1_attn_Wq, L1_attn_Wk, L1_attn_Wv, L1_attn_Wo,
           L1_attn_Wg, L1_attn_bg, L1_ffn_Wc, L1_ffn_bc, L1_ffn_W1, L1_ffn_b1,
           L1_ffn_W2, L1_ffn_b2):
    cores = list(range(NCORE))
    C = np.ascontiguousarray
    x = C(np.asarray(L0_attn_x, dtype=np.float32))
    ncf = _get("fused", build_fused)
    com = {
        "wqkv": C(np.asarray(L0_attn_weight_qkv, np.float32)),
        "owT": C(np.asarray(L0_attn_out_w, np.float32).T),
        "outb": C(np.asarray(L0_attn_out_b, np.float32)[None, :]),
        "fwT": C(np.asarray(L0_ffn_weights, np.float32).T),
        "fscale": C(np.asarray(L0_ffn_scale, np.float32)[None, :]),
        "fbias": C(np.asarray(L0_ffn_bias, np.float32)[None, :]),
        "wqT": C(np.asarray(L1_attn_Wq, np.float32).T),
        "wkT": C(np.asarray(L1_attn_Wk, np.float32).T),
        "wvT": C(np.asarray(L1_attn_Wv, np.float32).T),
        "wgT": C(np.asarray(L1_attn_Wg, np.float32).T),
        "bg": C(np.asarray(L1_attn_bg, np.float32)[:, None]),
        "woT": C(np.asarray(L1_attn_Wo, np.float32).T),
        "wc": C(np.asarray(L1_ffn_Wc, np.float32).reshape(D, 128)),
        "bc": C(np.asarray(L1_ffn_bc, np.float32).reshape(128, 1)),
        "w1": C(np.asarray(L1_ffn_W1, np.float32).reshape(128, D, 32)),
        "w2t": C(np.asarray(L1_ffn_W2, np.float32).transpose(0, 1, 3, 2).reshape(128, 32, D)),
        "b1": C(np.asarray(L1_ffn_b1, np.float32).reshape(128, 32)),
        "b2": C(np.asarray(L1_ffn_b2, np.float32).reshape(128, D)),
        "tri": np.where(np.arange(128)[:, None] <= np.arange(128)[None, :], 0.0, -1e9).astype(np.float32),
        "ident": np.eye(128, dtype=BF),
    }
    sel = np.zeros((4, 512), dtype=np.float32)
    for i in range(4):
        sel[i, i * 128:(i + 1) * 128] = 1.0
    com["sel"] = sel
    tie_pat = np.linspace(0.0, 1e-6, 32, dtype=np.float32)
    com["tie"] = C(np.broadcast_to(np.tile(tie_pat, 16)[None, :], (128, 512)).copy())
    ins = []
    for c in cores:
        b, p = divmod(c, 4)
        r0 = p * TPC
        d = dict(com)
        d["x"] = C(x[b, r0:r0 + TPC])
        ins.append(d)
    res = run_bass_kernel_spmd(ncf, ins, core_ids=cores).results
    outp = np.zeros((2, SQ, D), dtype=np.float32)
    for c in cores:
        b, p = divmod(c, 4)
        outp[b, p * TPC:(p + 1) * TPC] = res[c]["out"]
    return outp
